# version 04
# baseline (speedup 1.0000x reference)
import numpy as np
import concourse.bass as bass
import concourse.mybir as mybir
from concourse.bass_core_rust import IndirectOffsetOnAxis
from concourse.bass_utils import run_bass_kernel_spmd
from contextlib import ExitStack
from types import SimpleNamespace

F32 = mybir.dt.float32
BF16 = mybir.dt.bfloat16
U32 = mybir.dt.uint32
ALU = mybir.AluOpType
AF = mybir.ActivationFunctionType
AX = mybir.AxisListType

NCORES = 8
S = 2048
D = 1024
E = 16
HF = 4096
NH = 8
DH = 128
CAP = 1024
NTOK = NCORES * S
NT = S // 128
DC = D // 128
HC = HF // 128
EPS = 1e-5
BIGF = 1.0e6

GAMMA = 1.0 - np.exp(np.linspace(np.log(1.0 / 512.0), np.log(1.0 / 32.0), NH))
LNG = [float(np.log(np.float64(g))) for g in GAMMA]
GC = [float(np.float64(g) ** 128) for g in GAMMA]

_DBG = []
_DBG_RES = None
_CUT = None

_n = np.arange(128, dtype=np.float64)
_diff = _n[None, :] - _n[:, None]
DMASK_HOST = np.concatenate(
    [np.where(_diff >= 0, np.float64(g) ** np.maximum(_diff, 0), 0.0)
     for g in GAMMA], axis=1).astype(np.float32)
QDEC_HOST = np.stack([np.float64(g) ** (_n + 1) for g in GAMMA], 1).astype(np.float32)
KDEC_HOST = np.stack([np.float64(g) ** (127.0 - _n) for g in GAMMA], 1).astype(np.float32)



_RECORD = {
    "matmul", "transpose", "ldweights", "activation", "copy", "add", "mul",
    "memset", "memzero", "tensor_copy", "tensor_tensor", "tensor_add",
    "tensor_sub", "tensor_mul", "tensor_max", "tensor_scalar",
    "tensor_single_scalar", "tensor_scalar_add", "tensor_scalar_sub",
    "tensor_scalar_mul", "tensor_scalar_max", "tensor_scalar_min",
    "tensor_relu", "tensor_reduce", "reduce_max", "reduce_sum",
    "scalar_tensor_tensor", "reciprocal", "select", "iota", "affine_select",
    "max", "max_index", "bn_stats", "bn_aggr", "sign", "square", "sqrt",
}


class EngTrack:

    def __init__(self, eng):
        object.__setattr__(self, "_e", eng)
        object.__setattr__(self, "last", None)

    def __getattr__(self, name):
        a = getattr(self._e, name)
        if callable(a) and name in _RECORD:
            def wrap(*args, **kw):
                r = a(*args, **kw)
                if r is not None and hasattr(r, "then_inc"):
                    object.__setattr__(self, "last", r)
                return r
            return wrap
        if callable(a) and name == "max_with_indices":
            def wrapm(*args, **kw):
                r = a(*args, **kw)
                return r
            return wrapm
        return a


class BlockWrap:
    def __init__(self, blk):
        self._b = blk

    def _mk(self, reg):
        def dec(fn):
            return reg(lambda eng: fn(EngTrack(eng)))
        return dec

    @property
    def tensor(self):
        return self._mk(self._b.tensor)

    @property
    def vector(self):
        return self._mk(self._b.vector)

    @property
    def scalar(self):
        return self._mk(self._b.scalar)

    @property
    def gpsimd(self):
        return self._mk(self._b.gpsimd)

    @property
    def sync(self):
        return self._mk(self._b.sync)


ACT_WARMS = 64
ACT_WIDE = 512


def act_preload(s, func, K):
    s.preload_act_table(func)
    for _ in range(ACT_WARMS):
        s.activation(K.warm2[:], K.warm1[:], func)


def vgap(v, K):
    v.tensor_copy(K.gapt[:], K.iotacol1[:])


class Sem:
    def __init__(self, ctx, nc, name):
        self.h = ctx.enter_context(nc.semaphore(name))
        self.n = 0

    def inc(self, instr, k=1):
        instr.then_inc(self.h, k)
        self.n += k
        return instr

    def dma(self, instr):
        return self.inc(instr, 16)


def build(nc):
    K = SimpleNamespace()
    K.nc = nc
    ctx = ExitStack()
    K.ctx = ctx

    dt = nc.dram_tensor
    K.x_in = dt("x_in", [S, D], F32, kind="ExternalInput").ap()
    K.wq_in = dt("wq_in", [NH, DH, DH], F32, kind="ExternalInput").ap()
    K.wk_in = dt("wk_in", [NH, DH, DH], F32, kind="ExternalInput").ap()
    K.wv_in = dt("wv_in", [NH, DH, DH], F32, kind="ExternalInput").ap()
    K.w1a_in = dt("w1a_in", [D, D], F32, kind="ExternalInput").ap()
    K.w2a_in = dt("w2a_in", [D, D], F32, kind="ExternalInput").ap()
    K.gnw_in = dt("gnw_in", [128, DC], F32, kind="ExternalInput").ap()
    K.gnb_in = dt("gnb_in", [128, DC], F32, kind="ExternalInput").ap()
    K.wg_in = dt("wg_in", [D, E], F32, kind="ExternalInput").ap()
    K.bg_in = dt("bg_in", [1, E], F32, kind="ExternalInput").ap()
    K.myw1_in = dt("myw1_in", [2, D, HF], F32, kind="ExternalInput").ap()
    K.myb1_in = dt("myb1_in", [128, 2 * HC], F32, kind="ExternalInput").ap()
    K.myw2_in = dt("myw2_in", [2, HF, D], F32, kind="ExternalInput").ap()
    K.myb2_in = dt("myb2_in", [1, 2 * D], F32, kind="ExternalInput").ap()
    K.tokp1_in = dt("tokp1_in", [128, NT], F32, kind="ExternalInput").ap()
    K.coresel_in = dt("coresel_in", [NCORES, 1], F32, kind="ExternalInput").ap()
    K.cid_in = dt("cid_in", [1, 1], U32, kind="ExternalInput").ap()
    K.invsel_in = dt("invsel_in", [16, 1], U32, kind="ExternalInput").ap()
    K.dmask_in = dt("dmask_in", [128, NH * 128], F32, kind="ExternalInput").ap()
    K.qdec_in = dt("qdec_in", [128, NH], F32, kind="ExternalInput").ap()
    K.kdec_in = dt("kdec_in", [128, NH], F32, kind="ExternalInput").ap()
    K.out = dt("out", [S, D], F32, kind="ExternalOutput").ap()

    K.xres = dt("xres", [S, D], F32).ap()
    K.lc = dt("lc", [S, D], BF16).ap()
    K.gag = dt("gag", [NTOK, D], BF16).ap()
    K.cnt1 = dt("cnt1", [1, E], F32).ap()
    K.cnt8 = dt("cnt8", [NCORES, E], F32).ap()
    K.invd = dt("invd", [NTOK, 1], F32).ap()
    K.invr = dt("invr", [NTOK, 1], F32).ap()
    K.buftm = dt("buftm", [2 * CAP, D], BF16).ap()
    K.yd = dt("yd", [2 * CAP, D], BF16).ap()
    K.yag = dt("yag", [NTOK, D], BF16).ap()
    K.siltm = dt("siltm", [S, D], BF16).ap()
    K.rettm = dt("rettm", [S, D], BF16).ap()

    sb = lambda name, shape, dtp: ctx.enter_context(nc.sbuf_tensor(name, shape, dtp))

    K.iota16 = sb("iota16", [128, E], F32)
    K.upper_bf = sb("upper_bf", [128, 128], BF16)
    K.ident = sb("ident", [128, 128], F32)
    K.onesc_bf = sb("onesc_bf", [128, 1], BF16)
    K.ones1_bf = sb("ones1_bf", [1, 128], BF16)
    K.ones1_f = sb("ones1_f", [1, 128], F32)
    K.dmaskT = sb("dmaskT", [128, NH * 128], BF16)
    K.qdec = sb("qdec", [128, NH], F32)
    K.kdec = sb("kdec", [128, NH], F32)
    K.gcrow = sb("gcrow", [128, NH], F32)
    K.iotacol1 = sb("iotacol1", [128, 1], F32)
    K.iotacol2 = sb("iotacol2", [128, 1], F32)
    K.wg_bf = sb("wg_bf", [128, DC * E], BF16)
    K.bg_bf = sb("bg_bf", [1, E], BF16)
    K.wq_bf = sb("wq_bf", [128, NH * DH], BF16)
    K.wk_bf = sb("wk_bf", [128, NH * DH], BF16)
    K.wv_bf = sb("wv_bf", [128, NH * DH], BF16)
    K.gnw_col = sb("gnw_col", [128, DC], F32)
    K.gnb_bf = sb("gnb_bf", [128, DC], BF16)
    K.b1c = sb("b1c", [128, 2 * HC], F32)
    K.b2r_bf = sb("b2r_bf", [1, 2 * D], BF16)
    K.tokp1 = sb("tokp1", [128, NT], F32)
    K.coresel = sb("coresel", [NCORES, 1], F32)
    K.cid_sb = sb("cid_sb", [1, 1], U32)
    K.invsel = sb("invsel", [16, 1], U32)
    K.z128 = sb("z128", [128, 128], F32)
    K.gapt = sb("gapt", [128, 1], F32)
    K.gapt2 = sb("gapt2", [128, 1], F32)
    K.warm1 = sb("warm1", [128, 512], F32)
    K.warm2 = sb("warm2", [128, 512], F32)
    K.scr_i = sb("scr_i", [128, 128], F32)
    K.scr_m = sb("scr_m", [128, 128], F32)

    K.lg = sb("lg", [128, E], F32)
    K.lgall = sb("lgall", [128, NT * E], F32)
    K.negm = sb("negm", [128, NT], F32)
    K.esum = sb("esum", [128, NT], F32)
    K.eall = sb("eall", [128, NT * E], F32)
    K.eall2 = sb("eall2", [128, NT * E], F32)
    K.eo = sb("eo", [128, E], F32)
    K.gates = sb("gates", [128, NT], F32)
    K.idxf = sb("idxf", [128, NT], F32)
    K.iotamb = sb("iotamb", [128, E], F32)
    K.eqs = sb("eqs", [128, E], F32)
    K.ohbf = sb("ohbf", [128, NT * E], BF16)
    K.ohf = sb("ohf", [128, NT * E], F32)
    K.posg = sb("posg", [128, E], F32)
    K.poscol = sb("poscol", [128, 1], F32)
    K.dropf = sb("dropf", [128, 1], F32)
    K.slotf = sb("slotf", [128, NT], F32)
    K.slotu = sb("slotu", [128, NT], U32)
    K.gate_eff = sb("gate_eff", [128, NT], F32)
    K.base = sb("base", [1, E], F32)
    K.cnt_sb = sb("cnt_sb", [1, E], F32)
    K.cnt8_sb = sb("cnt8_sb", [NCORES, E], F32)
    K.inv16 = sb("inv16", [16, 128], F32)
    K.invT = sb("invT", [128, NT], F32)
    K.negc = sb("negc", [128, NT], F32)
    K.gidxu = sb("gidxu", [128, NT], U32)
    K.r0_bf = sb("r0_bf", [1, D], BF16)

    K.xt = [sb(f"xt{i}", [128, D], F32) for i in range(2)]
    K.xl = [sb(f"xl{i}", [128, D], F32) for i in range(2)]
    K.lnt = [sb(f"lnt{i}", [128, D], BF16) for i in range(2)]
    K.sqd = sb("sqd", [128, D], F32)
    K.srow = [sb(f"srow{i}", [128, 1], F32) for i in range(2)]
    K.ssq = [sb(f"ssq{i}", [128, 1], F32) for i in range(2)]
    K.mu = [sb(f"mu{i}", [128, 1], F32) for i in range(2)]
    K.t1 = [sb(f"t1_{i}", [128, 1], F32) for i in range(2)]
    K.t2 = [sb(f"t2_{i}", [128, 1], F32) for i in range(2)]
    K.rstd = [sb(f"rstd{i}", [128, 1], F32) for i in range(2)]
    K.bufstage = [sb(f"bufstage{i}", [128, D], BF16) for i in range(2)]
    K.rt = K.lnt
    K.yt = K.bufstage

    K.scrA = sb("scrA", [128, DC * S], BF16)
    K.scrB = sb("scrB", [128, HC * 512], BF16)
    K.scrC = sb("scrC", [128, HC * D], BF16)
    K.w1s = [sb(f"w1s{i}", [128, DC * 512], BF16) for i in range(2)]

    K.abar = sb("abar", [128, 512], BF16)
    K.S_sb = sb("S_sb", [128, 512], BF16)
    K.rets = K.sqd[:, 0:512]
    K.rsq = K.sqd[:, 512:1024]
    K.gnsum = sb("gnsum", [128, 128], F32)
    K.gnsq = sb("gnsq", [128, 128], F32)
    K.gnmu = sb("gnmu", [128, 128], F32)
    K.gnrs = sb("gnrs", [128, 128], F32)
    K.retgn = sb("retgn", [128, 512], BF16)
    K.muall = sb("muall", [128, NT], F32)
    K.vrall = sb("vrall", [128, NT], F32)
    K.rstall = sb("rstall", [128, NT], F32)
    K.nw1 = sb("nw1", [128, 128], F32)

    K.pA = ctx.enter_context(nc.psum_tensor("pA", [128, 1024], F32))
    K.pB = ctx.enter_context(nc.psum_tensor("pB", [128, 1024], F32))
    K.pC = ctx.enter_context(nc.psum_tensor("pC", [128, 1024], F32))
    K.pD = ctx.enter_context(nc.psum_tensor("pD", [128, 1024], F32))

    K.block = BlockWrap(ctx.enter_context(nc.Block()))

    K.sL = Sem(ctx, nc, "sL")
    K.sG = Sem(ctx, nc, "sG")
    K.sCC = Sem(ctx, nc, "sCC")
    K.sT = Sem(ctx, nc, "sT")
    K.sV = Sem(ctx, nc, "sV")
    K.sS = Sem(ctx, nc, "sS")
    K.sGc = Sem(ctx, nc, "sGc")

    K.dbg = {}
    for name in _DBG:
        shp = {"lc": [S, D], "gag": [NTOK, D], "invr": [NTOK, 1], "buftm": [2 * CAP, D],
               "yd": [2 * CAP, D], "x1": [S, D], "x2": [S, D], "slotu": [128, NT],
               "gates": [128, NT], "gidxu": [128, NT], "idxf": [128, NT],
               "siltm": [S, D], "rettm": [S, D], "geff": [128, NT]}[name]
        dtp = {"lc": BF16, "gag": BF16, "buftm": BF16, "yd": BF16, "slotu": U32,
               "gidxu": U32, "siltm": BF16, "rettm": BF16}.get(name, F32)
        K.dbg[name] = dt("dbg_" + name, shp, dtp, kind="ExternalOutput").ap()

    emit_setup(K)
    tr1 = emit_ffn(K, tag="f1", src=K.x_in, dst=K.xres)
    tr2 = emit_attn(K, tile_ready=tr1)
    emit_ffn(K, tag="f2", src=K.xres, dst=K.out, preload_exp=True, tile_ready=tr2)
    emit_finale(K)
    return K


def emit_setup(K):
    blk = K.block
    sL, sG, sV, sS, sGc = K.sL, K.sG, K.sV, K.sS, K.sGc

    @blk.sync
    def _(sy):
        sL.dma(sy.dma_start(out=K.tokp1[:], in_=K.tokp1_in))
        sL.dma(sy.dma_start(out=K.coresel[:], in_=K.coresel_in))
        sL.dma(sy.dma_start(out=K.cid_sb[:], in_=K.cid_in))
        sL.dma(sy.dma_start(out=K.invsel[:], in_=K.invsel_in))
        sL.dma(sy.dma_start(out=K.gnw_col[:], in_=K.gnw_in))
        sL.dma(sy.dma_start(out=K.b1c[:], in_=K.myb1_in))

    @blk.gpsimd
    def _(g):
        K.bnd = g.to_reg(NTOK - 1)
        sG.dma(g.dma_start(out=K.gnb_bf[:], in_=K.gnb_in))
        sG.dma(g.dma_start(out=K.b2r_bf[:], in_=K.myb2_in))
        sG.dma(g.dma_start(out=K.bg_bf[:], in_=K.bg_in))
        sG.dma(g.dma_start(
            out=K.wg_bf[:].rearrange("p (k e) -> p k e", k=DC),
            in_=K.wg_in.rearrange("(k p) e -> p k e", p=128)))
        for w_sb, w_in in ((K.wq_bf, K.wq_in), (K.wk_bf, K.wk_in), (K.wv_bf, K.wv_in)):
            sG.dma(g.dma_start(
                out=w_sb[:].rearrange("p (h j) -> p h j", h=NH),
                in_=w_in.rearrange("h p j -> p h j")))
        g.iota(K.iota16[:], pattern=[[1, E]], channel_multiplier=0,
               allow_small_or_imprecise_dtypes=True)
        g.iota(K.scr_i[:], pattern=[[1, 128]], channel_multiplier=-1,
               allow_small_or_imprecise_dtypes=True)
        g.iota(K.iotacol1[:], pattern=[[0, 1]], channel_multiplier=1, base=1,
               allow_small_or_imprecise_dtypes=True)
        g.iota(K.iotacol2[:], pattern=[[0, 1]], channel_multiplier=-1, base=127,
               allow_small_or_imprecise_dtypes=True)
        sGc.inc(g.last)

    @blk.vector
    def _(v):
        v.memset(K.onesc_bf[:], 1.0)
        v.memset(K.ones1_bf[:], 1.0)
        v.memset(K.ones1_f[:], 1.0)
        v.memset(K.z128[:], 0.0)
        v.memset(K.lnt[0][:], 0.0)
        v.memset(K.lnt[1][:], 0.0)
        v.memset(K.ident[:], 1.0)
        v.memset(K.scr_m[:], 1.0)
        for h in range(NH):
            v.memset(K.gcrow[:, h:h + 1], GC[h])
        sV.inc(v.last)

    @blk.gpsimd
    def _(g):
        g.wait_ge(sV.h, sV.n)
        g.affine_select(K.ident[:], K.ident[:], pattern=[[1, 128]],
                        compare_op=ALU.is_equal, fill=0.0, channel_multiplier=-1)
        g.affine_select(K.scr_m[:], K.scr_m[:], pattern=[[1, 128]],
                        compare_op=ALU.is_gt, fill=0.0, channel_multiplier=-1)
        sGc.inc(g.last)

    @blk.vector
    def _(v):
        v.wait_ge(sGc.h, sGc.n)
        v.tensor_copy(K.upper_bf[:], K.scr_m[:])
        v.tensor_scalar_sub(K.iotamb[:], K.iota16[:], 1000.0)
        sV.inc(v.last)

    @blk.gpsimd
    def _(g):
        sG.dma(g.dma_start(out=K.dmaskT[:], in_=K.dmask_in))

    @blk.sync
    def _(sy):
        sL.dma(sy.dma_start(out=K.qdec[:], in_=K.qdec_in))
        sL.dma(sy.dma_start(out=K.kdec[:], in_=K.kdec_in))

    @blk.vector
    def _(v):
        v.memset(K.warm1[:], 0.25)
        sV.inc(v.last)

    @blk.scalar
    def _(s):
        s.wait_ge(sV.h, sV.n)
        act_preload(s, AF.Exp, K)
        sS.inc(s.last)

    K.setup = {"g": K.sG.n, "s": K.sS.n, "v": K.sV.n, "gc": K.sGc.n, "l": K.sL.n}


def emit_rsqrt(v, K, dst, src, n, warr):
    I32 = mybir.dt.int32
    h = n // 2
    parts = [(dst[:, 0:h], src[:, 0:h], warr[:, 0:h]),
             (dst[:, h:n], src[:, h:n], warr[:, h:n])]
    for d, s_, w in parts:
        v.tensor_single_scalar(d.bitcast(I32), s_.bitcast(I32), 1,
                               op=ALU.arith_shift_right)
    for d, s_, w in parts:
        v.tensor_single_scalar(d.bitcast(I32), d.bitcast(I32), 0x5F3759DF,
                               op=ALU.subtract)
    for d, s_, w in parts:
        v.tensor_single_scalar(d.bitcast(I32), d.bitcast(I32), -1, op=ALU.mult)
    for _ in range(3):
        for d, s_, w in parts:
            v.tensor_tensor(w, d, d, op=ALU.mult)
        for d, s_, w in parts:
            v.tensor_tensor(w, w, s_, op=ALU.mult)
        for d, s_, w in parts:
            v.tensor_scalar(w, w, -0.5, 1.5, op0=ALU.mult, op1=ALU.add)
        for d, s_, w in parts:
            v.tensor_tensor(d, d, w, op=ALU.mult)


def emit_ln(K, src, order=None, tile_ready=None):
    blk = K.block
    sL, sV, sS = K.sL, K.sV, K.sS
    src_ready = sL.n
    vdone = [None, None]
    sdone = [None, None]
    ssq_cons = [None, None]
    for _j, i in enumerate(order or range(NT)):
        b = _j % 2

        @blk.sync
        def _(sy, i=i, b=b):
            sy.wait_ge(sL.h, src_ready if tile_ready is None else tile_ready[i])
            if vdone[b] is not None:
                sy.wait_ge(sV.h, vdone[b])
                sy.wait_ge(sS.h, sdone[b])
            sL.dma(sy.dma_start(out=K.xl[b][:], in_=src[i * 128:(i + 1) * 128, :]))

        ld_n = sL.n

        @blk.vector
        def _(v, b=b):
            v.wait_ge(sL.h, ld_n)
            v.reduce_sum(K.srow[b][:], K.xl[b][:], axis=AX.X)
            sV.inc(v.last)

        @blk.scalar
        def _(s, b=b):
            s.wait_ge(sL.h, ld_n)
            if ssq_cons[b] is not None:
                s.wait_ge(sV.h, ssq_cons[b])
            s.activation(K.sqd[:], K.xl[b][:], AF.Square, accum_out=K.ssq[b][:])
            sS.inc(s.last)

        sdone[b] = sS.n

        @blk.vector
        def _(v, b=b, i=i):
            v.wait_ge(sS.h, sS.n)
            v.tensor_scalar(K.t2[b][:], K.ssq[b][:], 1.0 / D, EPS,
                            op0=ALU.mult, op1=ALU.add)
            v.tensor_scalar_mul(K.muall[:, i:i + 1], K.srow[b][:], 1.0 / D)
            vgap(v, K)
            v.tensor_tensor(K.t1[b][:], K.muall[:, i:i + 1], K.muall[:, i:i + 1],
                            op=ALU.mult)
            vgap(v, K)
            v.tensor_tensor(K.vrall[:, i:i + 1], K.t2[b][:], K.t1[b][:],
                            op=ALU.subtract)
            sV.inc(v.last)
        ssq_cons[b] = sV.n
        vdone[b] = sV.n

    @blk.vector
    def _(v):
        emit_rsqrt(v, K, K.rstall[:], K.vrall[:], NT, K.nw1[:, 0:NT])
        sV.inc(v.last)

    lnt_free = [None, None]
    xfree = [sdone[0], sdone[1]]
    for _j, i in enumerate(order or range(NT)):
        b = _j % 2

        @blk.sync
        def _(sy, i=i, b=b):
            if tile_ready is not None:
                sy.wait_ge(sL.h, tile_ready[i])
            if vdone[b] is not None:
                sy.wait_ge(sV.h, vdone[b])
            sy.wait_ge(sS.h, xfree[b])
            sL.dma(sy.dma_start(out=K.xl[b][:], in_=src[i * 128:(i + 1) * 128, :]))

        ld_n = sL.n

        @blk.vector
        def _(v, b=b, i=i):
            v.wait_ge(sL.h, ld_n)
            if lnt_free[b] is not None:
                v.wait_ge(sL.h, lnt_free[b])
            v.tensor_scalar(K.lnt[b][:], K.xl[b][:], K.muall[:, i:i + 1],
                            K.rstall[:, i:i + 1], op0=ALU.subtract, op1=ALU.mult)
            sV.inc(v.last)

        vdone[b] = sV.n

        @blk.sync
        def _(sy, i=i, b=b):
            sy.wait_ge(sV.h, sV.n)
            sL.dma(sy.dma_start(out=K.lc[i * 128:(i + 1) * 128, :], in_=K.lnt[b][:]))

        lnt_free[b] = sL.n


def emit_ffn(K, tag, src, dst, preload_exp=False, tile_ready=None):
    blk = K.block
    sL, sG, sCC, sT, sV, sS = K.sL, K.sG, K.sCC, K.sT, K.sV, K.sS

    lnfm = K.scrA
    buf_fm = K.scrA
    hfm = K.scrB
    w2e = K.scrC[:, 0:HC * D]

    if _CUT == 'cut0':
        emit_dbg(K, tag)
        return
    emit_ln(K, src, tile_ready=tile_ready)
    lc_done = sL.n
    dst_tile_ready = [None] * NT
    if _CUT == 'ln0':
        emit_dbg(K, tag)
        return

    @blk.gpsimd
    def _(g):
        g.wait_ge(sL.h, lc_done)
        sCC.inc(g.collective_compute(
            "AllGather", ALU.bypass, replica_groups=[list(range(NCORES))],
            ins=[K.lc.opt()], outs=[K.gag.opt()]))
    ag_lc = sCC.n
    if _CUT == 'ag':
        emit_dbg(K, tag)
        return

    @blk.sync
    def _(sy):
        sL.dma(sy.dma_start(
            out=K.invd.rearrange("(a b) o -> a (b o)", a=128), in_=K.z128[:]))
    inv_zeroed = sL.n
    if _CUT == 'zero':
        emit_dbg(K, tag)
        return

    @blk.sync
    def _(sy):
        sy.wait_ge(sL.h, lc_done)
        for j in range(DC):
            sL.dma(sy.dma_start(out=lnfm[:, j * S:(j + 1) * S],
                                in_=K.lc[:, j * 128:(j + 1) * 128], transpose=True))
    lnfm_done = sL.n
    if _CUT == 'ln':
        emit_dbg(K, tag)
        return

    for tc in range(NT):
        @blk.tensor
        def _(t, tc=tc):
            t.wait_ge(sL.h, lnfm_done)
            if tc == 0:
                t.wait_ge(sG.h, K.setup["g"])
                t.wait_ge(K.sGc.h, K.setup["gc"])
            if tc > 0:
                t.wait_ge(sV.h, sV.n)
            for k in range(DC):
                t.matmul(K.pA[:, 0:E], lnfm[:, k * S + tc * 128:k * S + tc * 128 + 128],
                         K.wg_bf[:, k * E:(k + 1) * E], start=(k == 0), stop=False)
            sT.inc(t.matmul(K.pA[:, 0:E], K.ones1_bf[:], K.bg_bf[:],
                            start=False, stop=True))

        @blk.vector
        def _(v, tc=tc):
            v.wait_ge(sT.h, sT.n)
            v.tensor_copy(K.lgall[:, tc * E:(tc + 1) * E], K.pA[:, 0:E])
            vgap(v, K)
            lg = K.lgall[:, tc * E:(tc + 1) * E]
            v.reduce_max(K.mu[0][:], lg, axis=AX.X)
            vgap(v, K)
            v.tensor_tensor(K.eqs[:], lg,
                            K.mu[0][:, 0:1].broadcast_to((128, E)), op=ALU.is_equal)
            vgap(v, K)
            v.tensor_tensor(K.eqs[:], K.eqs[:], K.iotamb[:], op=ALU.mult)
            vgap(v, K)
            v.tensor_scalar_add(K.eqs[:], K.eqs[:], 1000.0)
            vgap(v, K)
            v.tensor_reduce(K.idxf[:, tc:tc + 1], K.eqs[:], axis=AX.X, op=ALU.min)
            vgap(v, K)
            v.tensor_tensor(K.ohbf[:, tc * E:(tc + 1) * E], K.iota16[:],
                            K.idxf[:, tc:tc + 1].broadcast_to((128, E)),
                            op=ALU.is_equal)
            v.tensor_tensor(K.ohf[:, tc * E:(tc + 1) * E], K.iota16[:],
                            K.idxf[:, tc:tc + 1].broadcast_to((128, E)),
                            op=ALU.is_equal)
            sV.inc(v.last)

        @blk.tensor
        def _(t, tc=tc):
            t.wait_ge(sV.h, sV.n)
            sT.inc(t.matmul(K.pB[0:1, 0:E], K.onesc_bf[:],
                            K.ohbf[:, tc * E:(tc + 1) * E],
                            start=(tc == 0), stop=(tc == NT - 1)))

    @blk.vector
    def _(v):
        v.tensor_reduce(K.negm[:].rearrange("p (a o) -> p a o", o=1),
                        K.lgall[:].rearrange("p (a e) -> p a e", a=NT),
                        axis=AX.X, op=ALU.max)
        vgap(v, K)
        v.tensor_tensor(K.eall[:].rearrange("p (a e) -> p a e", a=NT),
                        K.lgall[:].rearrange("p (a e) -> p a e", a=NT),
                        K.negm[:].rearrange("p (a o) -> p a o", o=1)
                            .broadcast_to((128, NT, E)),
                        op=ALU.subtract)
        sV.inc(v.last)

    @blk.scalar
    def _(s):
        if preload_exp:
            act_preload(s, AF.Exp, K)
        s.wait_ge(sV.h, sV.n)
        s.activation(K.eall2[:], K.eall[:], AF.Exp)
        sS.inc(s.last)

    @blk.vector
    def _(v):
        v.wait_ge(sS.h, sS.n)
        v.tensor_reduce(K.esum[:].rearrange("p (a o) -> p a o", o=1),
                        K.eall2[:].rearrange("p (a e) -> p a e", a=NT),
                        axis=AX.X, op=ALU.add)
        vgap(v, K)
        v.reciprocal(K.gates[:], K.esum[:])
        sV.inc(v.last)

    if _CUT == 'router':
        emit_dbg(K, tag)
        return
    @blk.vector
    def _(v):
        v.wait_ge(sT.h, sT.n)
        v.tensor_copy(K.cnt_sb[:], K.pB[0:1, 0:E])
        sV.inc(v.last)

    @blk.sync
    def _(sy):
        sy.wait_ge(sV.h, sV.n)
        sL.dma(sy.dma_start(out=K.cnt1, in_=K.cnt_sb[:]))
    cnt_st = sL.n

    @blk.gpsimd
    def _(g):
        g.wait_ge(sL.h, cnt_st)
        sCC.inc(g.collective_compute(
            "AllGather", ALU.bypass, replica_groups=[list(range(NCORES))],
            ins=[K.cnt1.opt()], outs=[K.cnt8.opt()]))

    @blk.sync
    def _(sy):
        sy.wait_ge(sCC.h, sCC.n)
        sL.dma(sy.dma_start(out=K.cnt8_sb[:], in_=K.cnt8))
    cnt8_ld = sL.n

    @blk.tensor
    def _(t):
        t.wait_ge(sL.h, cnt8_ld)
        sT.inc(t.matmul(K.pB[0:1, 0:E], K.coresel[:], K.cnt8_sb[:],
                        start=True, stop=True))

    @blk.vector
    def _(v):
        v.wait_ge(sT.h, sT.n)
        v.tensor_copy(K.base[:], K.pB[0:1, 0:E])
        sV.inc(v.last)

    for tc in range(NT):
        @blk.tensor
        def _(t, tc=tc):
            t.wait_ge(sV.h, sV.n)
            t.matmul(K.pA[:, 0:E], K.upper_bf[:], K.ohbf[:, tc * E:(tc + 1) * E],
                     start=True, stop=True)
            t.matmul(K.pC[:, 0:E], K.ones1_f[:], K.base[:], start=True, stop=True)
            sT.inc(t.matmul(K.pB[0:1, 0:E], K.onesc_bf[:],
                            K.ohbf[:, tc * E:(tc + 1) * E], start=True, stop=True))

        @blk.vector
        def _(v, tc=tc):
            v.wait_ge(sT.h, sT.n)
            v.tensor_copy(K.posg[:], K.pA[:, 0:E])
            v.tensor_tensor(K.base[:], K.base[:], K.pB[0:1, 0:E], op=ALU.add)
            v.tensor_tensor(K.posg[:], K.posg[:], K.pC[:, 0:E], op=ALU.add)
            vgap(v, K)
            v.tensor_tensor(K.posg[:], K.posg[:], K.ohf[:, tc * E:(tc + 1) * E],
                            op=ALU.mult)
            vgap(v, K)
            v.reduce_sum(K.poscol[:], K.posg[:], axis=AX.X)
            vgap(v, K)
            v.tensor_single_scalar(K.dropf[:], K.poscol[:], float(CAP), op=ALU.is_ge)
            v.scalar_tensor_tensor(K.slotf[:, tc:tc + 1], K.idxf[:, tc:tc + 1],
                                   float(CAP), K.poscol[:],
                                   op0=ALU.mult, op1=ALU.add)
            v.tensor_scalar(K.t1[0][:], K.dropf[:], -0.5, 0.5, op0=ALU.mult, op1=ALU.add)
            v.scalar_tensor_tensor(K.slotf[:, tc:tc + 1], K.dropf[:], BIGF,
                                   K.slotf[:, tc:tc + 1], op0=ALU.mult, op1=ALU.add)
            v.tensor_tensor(K.gate_eff[:, tc:tc + 1], K.t1[0][:],
                            K.gates[:, tc:tc + 1], op=ALU.mult)
            v.tensor_copy(K.slotu[:, tc:tc + 1], K.slotf[:, tc:tc + 1])
            sV.inc(v.last)
        slot_n = sV.n

        @blk.gpsimd
        def _(g, tc=tc):
            g.wait_ge(sV.h, slot_n)
            g.wait_ge(sL.h, inv_zeroed)
            sG.dma(g.indirect_dma_start(
                out=K.invd, out_offset=IndirectOffsetOnAxis(K.slotu[:, tc:tc + 1], 0),
                in_=K.tokp1[:, tc:tc + 1], in_offset=None,
                bounds_check=K.bnd, oob_is_err=False))

    if _CUT == 'prefix':
        emit_dbg(K, tag)
        return
    @blk.gpsimd
    def _(g):
        g.wait_ge(sG.h, sG.n)
        sCC.inc(g.collective_compute(
            "AllReduce", ALU.add, replica_groups=[list(range(NCORES))],
            ins=[K.invd.opt()], outs=[K.invr.opt()]))

    @blk.gpsimd
    def _(g):
        g.wait_ge(sCC.h, sCC.n)
        sG.dma(g.indirect_dma_start(
            out=K.inv16[:], out_offset=None,
            in_=K.invr.rearrange("(a b) o -> a (b o)", a=128),
            in_offset=IndirectOffsetOnAxis(K.invsel[:, 0:1], 0),
            bounds_check=K.bnd, oob_is_err=False))
    inv_ld = sG.n

    @blk.tensor
    def _(t):
        t.wait_ge(sG.h, inv_ld)
        t.wait_ge(K.sGc.h, K.setup["gc"])
        t.wait_ge(sV.h, sV.n)
        sT.inc(t.transpose(K.pA[:, 0:16], K.inv16[:], K.ident[0:16, 0:16]))

    @blk.vector
    def _(v):
        v.wait_ge(sT.h, sT.n)
        v.tensor_copy(K.invT[:], K.pA[:, 0:16])
        vgap(v, K)
        v.tensor_scalar_sub(K.invT[:], K.invT[:], 1.0)
        vgap(v, K)
        v.tensor_single_scalar(K.negc[:], K.invT[:], 0.0, op=ALU.is_lt)
        vgap(v, K)
        v.scalar_tensor_tensor(K.invT[:], K.negc[:], 2.0e9, K.invT[:],
                               op0=ALU.mult, op1=ALU.add)
        vgap(v, K)
        v.tensor_copy(K.gidxu[:], K.invT[:])
        sV.inc(v.last)
    gidx_done = sV.n
    if _CUT == 'inv':
        emit_dbg(K, tag)
        return

    stage_free = [None, None]
    for sbk in range(NT):
        b = sbk % 2

        @blk.gpsimd
        def _(g, sbk=sbk, b=b):
            g.wait_ge(sCC.h, ag_lc)
            g.wait_ge(sV.h, gidx_done)
            if stage_free[b] is not None:
                g.wait_ge(sL.h, stage_free[b])
            sG.dma(g.indirect_dma_start(
                out=K.bufstage[b][:], out_offset=None,
                in_=K.gag, in_offset=IndirectOffsetOnAxis(K.gidxu[:, sbk:sbk + 1], 0),
                bounds_check=K.bnd, oob_is_err=False))
        g_n = sG.n

        @blk.sync
        def _(sy, sbk=sbk, b=b):
            sy.wait_ge(sG.h, g_n)
            sL.dma(sy.dma_start(out=K.buftm[sbk * 128:(sbk + 1) * 128, :],
                                in_=K.bufstage[b][:]))
        stage_free[b] = sL.n
    buftm_done = sL.n
    if _CUT == 'gather':
        emit_dbg(K, tag)
        return

    @blk.sync
    def _(sy):
        sy.wait_ge(sL.h, buftm_done)
        for j in range(DC):
            sL.dma(sy.dma_start(out=buf_fm[:, j * 2 * CAP:(j + 1) * 2 * CAP],
                                in_=K.buftm[:, j * 128:(j + 1) * 128], transpose=True))
    buffm_done = sL.n
    if _CUT == 'xbar':
        emit_dbg(K, tag)
        return

    psum_l1 = [(K.pA, 0), (K.pA, 512), (K.pB, 0), (K.pB, 512)]
    psum_free = [sS.n] * 4
    w2_loaded = [None, None]
    w2_free = None
    w1_free = [None, None]
    yt_free = [None, None]
    for el in range(2):
        @blk.gpsimd
        def _(g, el=el):
            if el == 1 and w2_free is not None:
                g.wait_ge(sT.h, w2_free)
            for piece in range(4):
                sG.dma(g.dma_start(
                    out=w2e[:, piece * 8 * D:(piece + 1) * 8 * D]
                        .rearrange("p (hc d) -> p hc d", hc=8),
                    in_=K.myw2_in[el]
                        .rearrange("(hc p) d -> p hc d", p=128)[:, piece * 8:(piece + 1) * 8, :]))
        w2_loaded[el] = sG.n
        if _CUT == 'w2l' and el == 0:
            emit_dbg(K, tag)
            return

        for half in range(2):
            for hs in range(8):
                wbuf = hs % 2

                @blk.gpsimd
                def _(g, el=el, hs=hs, wbuf=wbuf):
                    if w1_free[wbuf] is not None:
                        g.wait_ge(sS.h, w1_free[wbuf])
                    sG.dma(g.dma_start(
                        out=K.w1s[wbuf][:].rearrange("p (k h) -> p k h", k=DC),
                        in_=K.myw1_in[el]
                            .rearrange("(k p) h -> p k h", p=128)[:, :, hs * 512:(hs + 1) * 512]))
                w1_ld = sG.n

                for hc4 in range(4):
                    ps, off = psum_l1[hc4]

                    @blk.tensor
                    def _(t, el=el, half=half, hs=hs, hc4=hc4, wbuf=wbuf, ps=ps, off=off,
                          w1_ld=w1_ld, pf=psum_free[hc4]):
                        t.wait_ge(sG.h, w1_ld)
                        t.wait_ge(sL.h, buffm_done)
                        t.wait_ge(sS.h, pf)
                        for k in range(DC):
                            t.matmul(
                                ps[:, off:off + 512],
                                K.w1s[wbuf][:, k * 512 + hc4 * 128:k * 512 + hc4 * 128 + 128],
                                buf_fm[:, k * 2 * CAP + el * CAP + half * 512:
                                       k * 2 * CAP + el * CAP + half * 512 + 512],
                                start=(k == 0), stop=(k == DC - 1))
                        sT.inc(t.last)

                    @blk.scalar
                    def _(s, el=el, hs=hs, hc4=hc4, ps=ps, off=off):
                        s.wait_ge(sT.h, sT.n)
                        hc = hs * 4 + hc4
                        s.activation(hfm[:, hc * 512:(hc + 1) * 512],
                                     ps[:, off:off + 512], AF.Relu,
                                     bias=K.b1c[:, el * HC + hc:el * HC + hc + 1])
                        sS.inc(s.last)

                    psum_free[hc4] = sS.n

                w1_free[wbuf] = sS.n
            l1_done = sS.n
            if _CUT == 'l1a' and el == 0 and half == 0:
                emit_dbg(K, tag)
                return

            for tc4 in range(4):
                b = tc4 % 2

                @blk.tensor
                def _(t, el=el, half=half, tc4=tc4, l1_done=l1_done):
                    t.wait_ge(sS.h, l1_done)
                    t.wait_ge(sG.h, w2_loaded[el])
                    t.wait_ge(sV.h, sV.n)
                    for dt2 in range(2):
                        for hc in range(HC):
                            t.matmul(
                                K.pC[:, dt2 * 512:dt2 * 512 + 512],
                                hfm[:, hc * 512 + tc4 * 128:hc * 512 + tc4 * 128 + 128],
                                w2e[:, hc * D + dt2 * 512:hc * D + dt2 * 512 + 512],
                                start=(hc == 0), stop=False)
                        t.matmul(K.pC[:, dt2 * 512:dt2 * 512 + 512], K.ones1_bf[:],
                                 K.b2r_bf[0:1, el * D + dt2 * 512:el * D + dt2 * 512 + 512],
                                 start=False, stop=True)
                    sT.inc(t.last)

                @blk.vector
                def _(v, tc4=tc4, b=b):
                    v.wait_ge(sT.h, sT.n)
                    if yt_free[b] is not None:
                        v.wait_ge(sL.h, yt_free[b])
                    v.tensor_copy(K.yt[b][:], K.pC[:])
                    sV.inc(v.last)

                @blk.sync
                def _(sy, el=el, half=half, tc4=tc4, b=b):
                    sy.wait_ge(sV.h, sV.n)
                    row = el * CAP + half * 512 + tc4 * 128
                    sL.dma(sy.dma_start(out=K.yd[row:row + 128, :], in_=K.yt[b][:]))
                yt_free[b] = sL.n
        w2_free = sT.n
        if _CUT == 'el0' and el == 0:
            emit_dbg(K, tag)
            return
    yd_done = sL.n
    if _CUT == 'exp':
        emit_dbg(K, tag)
        return

    @blk.gpsimd
    def _(g):
        g.wait_ge(sL.h, yd_done)
        sCC.inc(g.collective_compute(
            "AllGather", ALU.bypass, replica_groups=[list(range(NCORES))],
            ins=[K.yd.opt()], outs=[K.yag.opt()]))
    ag_y = sCC.n

    rt_free = [None, None]
    xn_free = [None, None]
    src_ready = sL.n
    for tc in range(NT):
        b = tc % 2

        @blk.gpsimd
        def _(g, tc=tc, b=b):
            g.wait_ge(sCC.h, ag_y)
            if rt_free[b] is not None:
                g.wait_ge(sV.h, rt_free[b])
            sG.dma(g.indirect_dma_start(
                out=K.rt[b][:], out_offset=None,
                in_=K.yag, in_offset=IndirectOffsetOnAxis(K.slotu[:, tc:tc + 1], 0),
                bounds_check=K.bnd, oob_is_err=False))
        g_n = sG.n

        @blk.sync
        def _(sy, tc=tc, b=b):
            sy.wait_ge(sL.h, src_ready)
            if xn_free[b] is not None:
                sy.wait_ge(sL.h, xn_free[b])
            sL.dma(sy.dma_start(out=K.xt[b][:], in_=src[tc * 128:(tc + 1) * 128, :]))
        l_n = sL.n

        @blk.vector
        def _(v, tc=tc, b=b, g_n=g_n, l_n=l_n):
            v.wait_ge(sG.h, g_n)
            v.wait_ge(sL.h, l_n)
            v.scalar_tensor_tensor(K.xt[b][:], K.rt[b][:],
                                   K.gate_eff[:, tc:tc + 1], K.xt[b][:],
                                   op0=ALU.mult, op1=ALU.add)
            sV.inc(v.last)
        rt_free[b] = sV.n

        @blk.sync
        def _(sy, tc=tc, b=b):
            sy.wait_ge(sV.h, sV.n)
            sL.dma(sy.dma_start(out=dst[tc * 128:(tc + 1) * 128, :], in_=K.xt[b][:]))
        xn_free[b] = sL.n
        dst_tile_ready[tc] = sL.n

    emit_dbg(K, tag)
    return dst_tile_ready


def emit_dbg(K, tag):
    if not K.dbg:
        return
    blk = K.block
    sL, sV = K.sL, K.sV
    want = {"f1": ["lc", "gag", "invr", "buftm", "yd", "slotu", "gates", "gidxu",
                   "idxf", "x1", "geff"],
            "f2": [], "at": ["siltm", "rettm", "x2"]}[tag]

    @blk.sync
    def _(sy):
        sy.wait_ge(sL.h, sL.n)
        sy.wait_ge(sV.h, sV.n)
        sy.wait_ge(K.sCC.h, K.sCC.n)
        sy.wait_ge(K.sG.h, K.sG.n)
        sy.wait_ge(K.sS.h, K.sS.n)
        sy.wait_ge(K.sT.h, K.sT.n)
        for name in want:
            if name not in K.dbg:
                continue
            if name in ("lc", "gag", "invr", "buftm", "yd", "siltm", "rettm"):
                sL.dma(sy.dma_start(out=K.dbg[name], in_=getattr(K, name)))
            elif name in ("x1", "x2"):
                sL.dma(sy.dma_start(out=K.dbg[name], in_=K.xres))
            elif name == "geff":
                sL.dma(sy.dma_start(out=K.dbg[name], in_=K.gate_eff[:]))
            else:
                sL.dma(sy.dma_start(out=K.dbg[name], in_=getattr(K, name)[:]))
        sy.wait_ge(sL.h, sL.n)


def emit_attn(K, tile_ready=None):
    if _CUT is not None:
        return None
    blk = K.block
    sL, sG, sT, sV, sS = K.sL, K.sG, K.sT, K.sV, K.sS

    x_fm = K.scrA
    w1a = K.scrC[:, 0:DC * D]
    w2a = K.scrC[:, DC * D:2 * DC * D]
    slot3 = K.scrC[:, 2 * DC * D:3 * DC * D]
    vtm = K.scrC[:, 0:DC * D]
    kttm = slot3
    w2p = slot3
    qfm = K.scrB[:, 0:4 * S]
    kfm = K.scrB[:, 4 * S:8 * S]

    emit_ln(K, K.xres, tile_ready=tile_ready)
    lc_done = sL.n
    dst_tile_ready = [None] * NT

    @blk.gpsimd
    def _(g):
        sG.dma(g.dma_start(out=w1a.rearrange("p (k j) -> p k j", k=DC),
                           in_=K.w1a_in.rearrange("(k p) j -> p k j", p=128)))
        sG.dma(g.dma_start(out=w2a.rearrange("p (k j) -> p k j", k=DC),
                           in_=K.w2a_in.rearrange("(k p) j -> p k j", p=128)))
    wa_done = sG.n

    @blk.tensor
    def _(t):
        t.wait_ge(sG.h, wa_done)
        t.wait_ge(sV.h, sV.n)
        for dt2 in range(2):
            for k in range(DC):
                t.matmul(K.pD[0:1, dt2 * 512:dt2 * 512 + 512],
                         K.gnb_bf[:, k:k + 1],
                         w2a[:, k * D + dt2 * 512:k * D + dt2 * 512 + 512],
                         start=(k == 0), stop=(k == DC - 1))
        sT.inc(t.last)

    @blk.vector
    def _(v):
        v.wait_ge(sT.h, sT.n)
        v.tensor_copy(K.r0_bf[:], K.pD[0:1, :])
        sV.inc(v.last)

    @blk.sync
    def _(sy):
        sy.wait_ge(sL.h, lc_done)
        for j in range(DC):
            sL.dma(sy.dma_start(out=x_fm[:, j * S:(j + 1) * S],
                                in_=K.lc[:, j * 128:(j + 1) * 128], transpose=True))
    xfm_done = sL.n

    silt_free = [None, None]
    for tc in range(NT):
        b = tc % 2

        @blk.tensor
        def _(t, tc=tc):
            t.wait_ge(sL.h, xfm_done)
            if tc == 0:
                t.wait_ge(sG.h, wa_done)
            t.wait_ge(sV.h, sV.n)
            for dt2 in range(2):
                for k in range(DC):
                    t.matmul(K.pA[:, dt2 * 512:dt2 * 512 + 512],
                             x_fm[:, k * S + tc * 128:k * S + tc * 128 + 128],
                             w1a[:, k * D + dt2 * 512:k * D + dt2 * 512 + 512],
                             start=(k == 0), stop=(k == DC - 1))
            sT.inc(t.last)

        @blk.scalar
        def _(s, tc=tc):
            if tc == 0:
                act_preload(s, AF.Sigmoid, K)
            s.wait_ge(sT.h, sT.n)
            s.wait_ge(sV.h, sV.n)
            s.activation(K.sqd[:], K.pA[:], AF.Sigmoid)
            sS.inc(s.last)

        @blk.vector
        def _(v, tc=tc, b=b):
            v.wait_ge(sS.h, sS.n)
            if silt_free[b] is not None:
                v.wait_ge(sL.h, silt_free[b])
            v.tensor_tensor(K.lnt[b][:], K.sqd[:], K.pA[:], op=ALU.mult)
            sV.inc(v.last)

        @blk.sync
        def _(sy, tc=tc, b=b):
            sy.wait_ge(sV.h, sV.n)
            sL.dma(sy.dma_start(out=K.siltm[tc * 128:(tc + 1) * 128, :], in_=K.lnt[b][:]))
        silt_free[b] = sL.n
    gated_mm_done = sT.n

    emit_retention(K, x_fm, qfm, kfm, vtm, kttm, gated_mm_done)
    retention_T_done = sT.n

    @blk.vector
    def _(v):
        v.wait_ge(sT.h, retention_T_done)
        for k in range(DC):
            v.tensor_scalar_mul(w2p[:, k * D:(k + 1) * D], w2a[:, k * D:(k + 1) * D],
                                K.gnw_col[:, k:k + 1])
        sV.inc(v.last)
    w2p_done = sV.n

    silu_fm = K.scrA
    retgn_fm = K.scrB

    @blk.sync
    def _(sy):
        sy.wait_ge(sL.h, sL.n)
        sy.wait_ge(sT.h, retention_T_done)
        for j in range(DC):
            sL.dma(sy.dma_start(out=silu_fm[:, j * S:(j + 1) * S],
                                in_=K.siltm[:, j * 128:(j + 1) * 128], transpose=True))
        for j in range(DC):
            sL.dma(sy.dma_start(out=retgn_fm[:, j * S:(j + 1) * S],
                                in_=K.rettm[:, j * 128:(j + 1) * 128], transpose=True))
    fm_done = sL.n

    xn_free = [None, None]
    src_ready = sL.n
    for tc in range(NT):
        b = tc % 2

        @blk.tensor
        def _(t, tc=tc):
            t.wait_ge(sL.h, fm_done)
            t.wait_ge(sV.h, sV.n)
            if tc == 0:
                t.wait_ge(K.sS.h, K.sS.n)
                t.wait_ge(sV.h, w2p_done)
            for dt2 in range(2):
                for k in range(DC):
                    t.matmul(K.pA[:, dt2 * 512:dt2 * 512 + 512],
                             silu_fm[:, k * S + tc * 128:k * S + tc * 128 + 128],
                             w2a[:, k * D + dt2 * 512:k * D + dt2 * 512 + 512],
                             start=(k == 0), stop=False)
                    t.matmul(K.pA[:, dt2 * 512:dt2 * 512 + 512],
                             retgn_fm[:, k * S + tc * 128:k * S + tc * 128 + 128],
                             w2p[:, k * D + dt2 * 512:k * D + dt2 * 512 + 512],
                             start=False, stop=False)
                t.matmul(K.pA[:, dt2 * 512:dt2 * 512 + 512], K.ones1_bf[:],
                         K.r0_bf[0:1, dt2 * 512:dt2 * 512 + 512],
                         start=False, stop=True)
            sT.inc(t.last)

        @blk.sync
        def _(sy, tc=tc, b=b):
            sy.wait_ge(sL.h, src_ready)
            if xn_free[b] is not None:
                sy.wait_ge(sL.h, xn_free[b])
            sL.dma(sy.dma_start(out=K.xt[b][:], in_=K.xres[tc * 128:(tc + 1) * 128, :]))
        l_n = sL.n

        @blk.vector
        def _(v, tc=tc, b=b, l_n=l_n):
            v.wait_ge(sT.h, sT.n)
            v.wait_ge(sL.h, l_n)
            v.tensor_tensor(K.xt[b][:], K.pA[:], K.xt[b][:], op=ALU.add)
            sV.inc(v.last)

        @blk.sync
        def _(sy, tc=tc, b=b):
            sy.wait_ge(sV.h, sV.n)
            sL.dma(sy.dma_start(out=K.xres[tc * 128:(tc + 1) * 128, :], in_=K.xt[b][:]))
        xn_free[b] = sL.n
        dst_tile_ready[tc] = sL.n

    emit_dbg(K, "at")
    return dst_tile_ready


def emit_retention(K, x_fm, qfm, kfm, vtm, kttm, gated_mm_done):
    blk = K.block
    sL, sT, sV, sS = K.sL, K.sT, K.sV, K.sS
    silu_evacs_done = sS.n

    for grp in range(2):
        heads = [grp * 4 + i for i in range(4)]

        for i4, h in enumerate(heads):
            for n in range(4):
                @blk.tensor
                def _(t, h=h, n=n, i4=i4):
                    if i4 == 0 and n == 0:
                        t.wait_ge(K.sG.h, K.setup["g"])
                        t.wait_ge(sS.h, K.setup["s"])
                    t.wait_ge(sV.h, sV.n)
                    t.wait_ge(sS.h, silu_evacs_done)
                    t.matmul(K.pA[:, 0:512], K.wq_bf[:, h * DH:(h + 1) * DH],
                             x_fm[:, h * S + n * 512:h * S + n * 512 + 512],
                             start=True, stop=True)
                    t.matmul(K.pA[:, 512:1024], K.wk_bf[:, h * DH:(h + 1) * DH],
                             x_fm[:, h * S + n * 512:h * S + n * 512 + 512],
                             start=True, stop=True)
                    sT.inc(t.last)

                @blk.vector
                def _(v, i4=i4, n=n):
                    v.wait_ge(sT.h, sT.n)
                    v.tensor_copy(qfm[:, i4 * S + n * 512:i4 * S + n * 512 + 512],
                                  K.pA[:, 0:512])
                    v.tensor_copy(kfm[:, i4 * S + n * 512:i4 * S + n * 512 + 512],
                                  K.pA[:, 512:1024])
                    sV.inc(v.last)

        for i4, h in enumerate(heads):
            for tc in range(NT):
                @blk.tensor
                def _(t, h=h, tc=tc):
                    t.wait_ge(sV.h, sV.n)
                    t.matmul(K.pB[:, 0:128],
                             x_fm[:, h * S + tc * 128:h * S + tc * 128 + 128],
                             K.wv_bf[:, h * DH:(h + 1) * DH], start=True, stop=True)
                    t.matmul(K.pB[:, 128:256],
                             x_fm[:, h * S + tc * 128:h * S + tc * 128 + 128],
                             K.wk_bf[:, h * DH:(h + 1) * DH], start=True, stop=True)
                    sT.inc(t.last)

                @blk.vector
                def _(v, h=h, i4=i4, tc=tc):
                    v.wait_ge(sT.h, sT.n)
                    v.tensor_copy(vtm[:, i4 * S + tc * 128:i4 * S + tc * 128 + 128],
                                  K.pB[:, 0:128])
                    v.tensor_scalar_mul(
                        kttm[:, i4 * S + tc * 128:i4 * S + tc * 128 + 128],
                        K.pB[:, 128:256], K.kdec[:, h:h + 1])
                    sV.inc(v.last)

        retgn_free = [None]

        @blk.vector
        def _(v):
            v.memset(K.S_sb[:], 0.0)
            sV.inc(v.last)

        for ci in range(NT):
            @blk.tensor
            def _(t, ci=ci, grp=grp):
                t.wait_ge(sV.h, sV.n)
                for i4 in range(4):
                    t.matmul(K.pA[:, i4 * 128:(i4 + 1) * 128],
                             kfm[:, i4 * S + ci * 128:i4 * S + ci * 128 + 128],
                             qfm[:, i4 * S + ci * 128:i4 * S + ci * 128 + 128],
                             start=True, stop=True)
                sT.inc(t.last)

            @blk.vector
            def _(v, ci=ci, grp=grp):
                v.wait_ge(sT.h, sT.n)
                v.tensor_tensor(K.abar[:], K.pA[:, 0:512],
                                K.dmaskT[:, grp * 512:(grp + 1) * 512], op=ALU.mult)
                sV.inc(v.last)
            abar_n = sV.n

            @blk.tensor
            def _(t, ci=ci, abar_n=abar_n):
                t.wait_ge(sV.h, abar_n)
                for i4 in range(4):
                    t.matmul(K.pB[:, i4 * 128:(i4 + 1) * 128],
                             K.abar[:, i4 * 128:(i4 + 1) * 128],
                             vtm[:, i4 * S + ci * 128:i4 * S + ci * 128 + 128],
                             start=True, stop=True)
                    t.matmul(K.pC[:, i4 * 128:(i4 + 1) * 128],
                             qfm[:, i4 * S + ci * 128:i4 * S + ci * 128 + 128],
                             K.S_sb[:, i4 * 128:(i4 + 1) * 128],
                             start=True, stop=True)
                    t.matmul(K.pD[:, i4 * 128:(i4 + 1) * 128],
                             kttm[:, i4 * S + ci * 128:i4 * S + ci * 128 + 128],
                             vtm[:, i4 * S + ci * 128:i4 * S + ci * 128 + 128],
                             start=True, stop=True)
                sT.inc(t.last)
            ret_mm = sT.n

            @blk.vector
            def _(v, ci=ci, grp=grp, ret_mm=ret_mm):
                v.wait_ge(sT.h, ret_mm)
                v.tensor_tensor(
                    K.rets.rearrange("p (a b) -> p a b", a=4),
                    K.pC[:, 0:512].rearrange("p (a b) -> p a b", a=4),
                    K.qdec[:, grp * 4:(grp + 1) * 4]
                        .rearrange("p (a o) -> p a o", o=1).broadcast_to((128, 4, 128)),
                    op=ALU.mult)
                v.tensor_tensor(
                    K.rsq.rearrange("p (a b) -> p a b", a=4),
                    K.S_sb[:].rearrange("p (a b) -> p a b", a=4),
                    K.gcrow[:, grp * 4:(grp + 1) * 4]
                        .rearrange("p (a o) -> p a o", o=1).broadcast_to((128, 4, 128)),
                    op=ALU.mult)
                v.tensor_tensor(K.rets, K.rets, K.pB[:, 0:512], op=ALU.add)
                v.tensor_tensor(K.S_sb[:], K.rsq, K.pD[:, 0:512], op=ALU.add)
                v.tensor_reduce(
                    K.gnsum[:, ci * 8 + grp * 4:ci * 8 + grp * 4 + 4]
                        .rearrange("p (a o) -> p a o", o=1),
                    K.rets.rearrange("p (a b) -> p a b", a=4),
                    axis=AX.X, op=ALU.add)
                sV.inc(v.last)

            @blk.scalar
            def _(s, ci=ci, grp=grp):
                s.wait_ge(sV.h, sV.n)
                s.activation(K.rsq, K.rets, AF.Square)
                sS.inc(s.last)

            @blk.vector
            def _(v, ci=ci, grp=grp):
                v.wait_ge(sS.h, sS.n)
                if retgn_free[0] is not None:
                    v.wait_ge(sL.h, retgn_free[0])
                v.tensor_reduce(
                    K.gnsq[:, ci * 8 + grp * 4:ci * 8 + grp * 4 + 4]
                        .rearrange("p (a o) -> p a o", o=1),
                    K.rsq.rearrange("p (a b) -> p a b", a=4),
                    axis=AX.X, op=ALU.add)
                v.tensor_copy(K.retgn[:], K.rets)
                sV.inc(v.last)
            gn_n = sV.n

            @blk.sync
            def _(sy, ci=ci, grp=grp, gn_n=gn_n):
                sy.wait_ge(sV.h, gn_n)
                sL.dma(sy.dma_start(
                    out=K.rettm[ci * 128:(ci + 1) * 128, grp * 512:(grp + 1) * 512],
                    in_=K.retgn[:]))
            retgn_free[0] = sL.n

    @blk.vector
    def _(v):
        v.wait_ge(sL.h, sL.n)
        v.tensor_scalar_mul(K.gnmu[:], K.gnsum[:], 1.0 / DH)
        v.tensor_scalar(K.gnsq[:], K.gnsq[:], 1.0 / DH, EPS,
                        op0=ALU.mult, op1=ALU.add)
        v.tensor_tensor(K.gnsum[:], K.gnmu[:], K.gnmu[:], op=ALU.mult)
        vgap(v, K)
        v.tensor_tensor(K.gnsq[:], K.gnsq[:], K.gnsum[:], op=ALU.subtract)
        vgap(v, K)
        emit_rsqrt(v, K, K.gnrs[:], K.gnsq[:], 128, K.nw1[:])
        sV.inc(v.last)
    gnstats_done = sV.n

    lnt_free2 = [None, None]
    for i in range(NT):
        b = i % 2

        @blk.sync
        def _(sy, i=i, b=b):
            sy.wait_ge(sV.h, gnstats_done)
            if lnt_free2[b] is not None:
                sy.wait_ge(sL.h, lnt_free2[b])
            sL.dma(sy.dma_start(out=K.lnt[b][:], in_=K.rettm[i * 128:(i + 1) * 128, :]))
        ld_n = sL.n

        @blk.vector
        def _(v, i=i, b=b, ld_n=ld_n):
            v.wait_ge(sL.h, ld_n)
            v.tensor_tensor(
                K.lnt[b][:].rearrange("p (a b2) -> p a b2", a=8),
                K.lnt[b][:].rearrange("p (a b2) -> p a b2", a=8),
                K.gnmu[:, i * 8:(i + 1) * 8].rearrange("p (a o) -> p a o", o=1)
                    .broadcast_to((128, 8, 128)),
                op=ALU.subtract)
            vgap(v, K)
            v.tensor_tensor(
                K.lnt[b][:].rearrange("p (a b2) -> p a b2", a=8),
                K.lnt[b][:].rearrange("p (a b2) -> p a b2", a=8),
                K.gnrs[:, i * 8:(i + 1) * 8].rearrange("p (a o) -> p a o", o=1)
                    .broadcast_to((128, 8, 128)),
                op=ALU.mult)
            sV.inc(v.last)

        @blk.sync
        def _(sy, i=i, b=b):
            sy.wait_ge(sV.h, sV.n)
            sL.dma(sy.dma_start(out=K.rettm[i * 128:(i + 1) * 128, :], in_=K.lnt[b][:]))
        lnt_free2[b] = sL.n


def emit_finale(K):
    blk = K.block
    sL = K.sL

    @blk.sync
    def _(sy):
        sy.wait_ge(sL.h, sL.n)


def kernel(L0_attn_x, L0_attn_Wq, L0_attn_Wk, L0_attn_Wv, L0_attn_W1, L0_attn_W2,
           L0_attn_gn_w, L0_attn_gn_b, L0_ffn_Wg, L0_ffn_bg, L0_ffn_W1, L0_ffn_b1,
           L0_ffn_W2, L0_ffn_b2):
    global _DBG_RES
    x = np.ascontiguousarray(np.asarray(L0_attn_x, dtype=np.float32))
    wq = np.ascontiguousarray(np.asarray(L0_attn_Wq, np.float32))
    wk = np.ascontiguousarray(np.asarray(L0_attn_Wk, np.float32))
    wv = np.ascontiguousarray(np.asarray(L0_attn_Wv, np.float32))
    w1a = np.ascontiguousarray(np.asarray(L0_attn_W1, np.float32))
    w2a = np.ascontiguousarray(np.asarray(L0_attn_W2, np.float32))
    gnw = np.asarray(L0_attn_gn_w, np.float32)
    gnb = np.asarray(L0_attn_gn_b, np.float32)
    wg = np.ascontiguousarray(np.asarray(L0_ffn_Wg, np.float32))
    bg = np.asarray(L0_ffn_bg, np.float32)
    w1 = np.ascontiguousarray(np.asarray(L0_ffn_W1, np.float32))
    b1 = np.asarray(L0_ffn_b1, np.float32)
    w2 = np.ascontiguousarray(np.asarray(L0_ffn_W2, np.float32))
    b2 = np.asarray(L0_ffn_b2, np.float32)

    nc = bass.Bass("TRN2", target_bir_lowering=False, detect_race_conditions=False)
    build(nc)

    gnw_t = np.ascontiguousarray(gnw.reshape(DC, 128).T)
    gnb_t = np.ascontiguousarray(gnb.reshape(DC, 128).T)
    bg_t = np.ascontiguousarray(bg.reshape(1, E))

    in_maps = []
    for c in range(NCORES):
        b1_t = np.ascontiguousarray(
            b1[2 * c:2 * c + 2].reshape(2, HC, 128).transpose(2, 0, 1).reshape(128, 2 * HC))
        b2_t = np.ascontiguousarray(b2[2 * c:2 * c + 2].reshape(1, 2 * D))
        tokp1 = np.ascontiguousarray(
            (np.arange(S).reshape(NT, 128).T + 2048.0 * c + 1.0).astype(np.float32))
        coresel = (np.arange(NCORES) < c).astype(np.float32).reshape(NCORES, 1)
        in_maps.append({
            "x_in": np.ascontiguousarray(x[c]),
            "wq_in": wq, "wk_in": wk, "wv_in": wv,
            "w1a_in": w1a, "w2a_in": w2a,
            "gnw_in": gnw_t, "gnb_in": gnb_t,
            "wg_in": wg, "bg_in": bg_t,
            "myw1_in": np.ascontiguousarray(w1[2 * c:2 * c + 2]),
            "myb1_in": b1_t,
            "myw2_in": np.ascontiguousarray(w2[2 * c:2 * c + 2]),
            "myb2_in": b2_t,
            "tokp1_in": tokp1,
            "coresel_in": coresel,
            "cid_in": np.array([[c]], dtype=np.uint32),
            "invsel_in": (16 * c + np.arange(16, dtype=np.uint32)).reshape(16, 1),
            "dmask_in": DMASK_HOST, "qdec_in": QDEC_HOST, "kdec_in": KDEC_HOST,
        })

    res = run_bass_kernel_spmd(nc, in_maps, core_ids=list(range(NCORES)))
    if _DBG:
        _DBG_RES = [{k: r.get("dbg_" + k) for k in _DBG} for r in res.results]
    out = np.stack([res.results[c]["out"] for c in range(NCORES)], axis=0)
    return out.astype(np.float32)


# version 05 (final)
# speedup vs baseline: 1.0132x; 1.0132x over previous
import numpy as np
import concourse.bass as bass
import concourse.mybir as mybir
from concourse.bass_core_rust import IndirectOffsetOnAxis
from concourse.bass_utils import run_bass_kernel_spmd
from contextlib import ExitStack
from types import SimpleNamespace

F32 = mybir.dt.float32
BF16 = mybir.dt.bfloat16
U32 = mybir.dt.uint32
ALU = mybir.AluOpType
AF = mybir.ActivationFunctionType
AX = mybir.AxisListType

NCORES = 8
S = 2048
D = 1024
E = 16
HF = 4096
NH = 8
DH = 128
CAP = 1024
NTOK = NCORES * S
NT = S // 128
DC = D // 128
HC = HF // 128
EPS = 1e-5
BIGF = 1.0e6

GAMMA = 1.0 - np.exp(np.linspace(np.log(1.0 / 512.0), np.log(1.0 / 32.0), NH))
LNG = [float(np.log(np.float64(g))) for g in GAMMA]
GC = [float(np.float64(g) ** 128) for g in GAMMA]

_DBG = []
_DBG_RES = None
_CUT = None

_n = np.arange(128, dtype=np.float64)
_diff = _n[None, :] - _n[:, None]
DMASK_HOST = np.concatenate(
    [np.where(_diff >= 0, np.float64(g) ** np.maximum(_diff, 0), 0.0)
     for g in GAMMA], axis=1).astype(np.float32)
QDEC_HOST = np.stack([np.float64(g) ** (_n + 1) for g in GAMMA], 1).astype(np.float32)
KDEC_HOST = np.stack([np.float64(g) ** (127.0 - _n) for g in GAMMA], 1).astype(np.float32)



_RECORD = {
    "matmul", "transpose", "ldweights", "activation", "copy", "add", "mul",
    "memset", "memzero", "tensor_copy", "tensor_tensor", "tensor_add",
    "tensor_sub", "tensor_mul", "tensor_max", "tensor_scalar",
    "tensor_single_scalar", "tensor_scalar_add", "tensor_scalar_sub",
    "tensor_scalar_mul", "tensor_scalar_max", "tensor_scalar_min",
    "tensor_relu", "tensor_reduce", "reduce_max", "reduce_sum",
    "scalar_tensor_tensor", "reciprocal", "select", "iota", "affine_select",
    "max", "max_index", "bn_stats", "bn_aggr", "sign", "square", "sqrt",
}


class EngTrack:

    def __init__(self, eng):
        object.__setattr__(self, "_e", eng)
        object.__setattr__(self, "last", None)

    def __getattr__(self, name):
        a = getattr(self._e, name)
        if callable(a) and name in _RECORD:
            def wrap(*args, **kw):
                r = a(*args, **kw)
                if r is not None and hasattr(r, "then_inc"):
                    object.__setattr__(self, "last", r)
                return r
            return wrap
        if callable(a) and name == "max_with_indices":
            def wrapm(*args, **kw):
                r = a(*args, **kw)
                return r
            return wrapm
        return a


class BlockWrap:
    def __init__(self, blk):
        self._b = blk

    def _mk(self, reg):
        def dec(fn):
            return reg(lambda eng: fn(EngTrack(eng)))
        return dec

    @property
    def tensor(self):
        return self._mk(self._b.tensor)

    @property
    def vector(self):
        return self._mk(self._b.vector)

    @property
    def scalar(self):
        return self._mk(self._b.scalar)

    @property
    def gpsimd(self):
        return self._mk(self._b.gpsimd)

    @property
    def sync(self):
        return self._mk(self._b.sync)


ACT_WARMS = 64
ACT_WIDE = 512


def act_preload(s, func, K):
    s.preload_act_table(func)
    for _ in range(ACT_WARMS):
        s.activation(K.warm2[:], K.warm1[:], func)


def vgap(v, K):
    v.tensor_copy(K.gapt[:], K.iotacol1[:])


class Sem:
    def __init__(self, ctx, nc, name):
        self.h = ctx.enter_context(nc.semaphore(name))
        self.n = 0

    def inc(self, instr, k=1):
        instr.then_inc(self.h, k)
        self.n += k
        return instr

    def dma(self, instr):
        return self.inc(instr, 16)


def build(nc):
    K = SimpleNamespace()
    K.nc = nc
    ctx = ExitStack()
    K.ctx = ctx

    dt = nc.dram_tensor
    K.x_in = dt("x_in", [S, D], F32, kind="ExternalInput").ap()
    K.wq_in = dt("wq_in", [NH, DH, DH], F32, kind="ExternalInput").ap()
    K.wk_in = dt("wk_in", [NH, DH, DH], F32, kind="ExternalInput").ap()
    K.wv_in = dt("wv_in", [NH, DH, DH], F32, kind="ExternalInput").ap()
    K.w1a_in = dt("w1a_in", [D, D], F32, kind="ExternalInput").ap()
    K.w2a_in = dt("w2a_in", [D, D], F32, kind="ExternalInput").ap()
    K.gnw_in = dt("gnw_in", [128, DC], F32, kind="ExternalInput").ap()
    K.gnb_in = dt("gnb_in", [128, DC], F32, kind="ExternalInput").ap()
    K.wg_in = dt("wg_in", [D, E], F32, kind="ExternalInput").ap()
    K.bg_in = dt("bg_in", [1, E], F32, kind="ExternalInput").ap()
    K.myw1_in = dt("myw1_in", [2, D, HF], F32, kind="ExternalInput").ap()
    K.myb1_in = dt("myb1_in", [128, 2 * HC], F32, kind="ExternalInput").ap()
    K.myw2_in = dt("myw2_in", [2, HF, D], F32, kind="ExternalInput").ap()
    K.myb2_in = dt("myb2_in", [1, 2 * D], F32, kind="ExternalInput").ap()
    K.tokp1_in = dt("tokp1_in", [128, NT], F32, kind="ExternalInput").ap()
    K.coresel_in = dt("coresel_in", [NCORES, 1], F32, kind="ExternalInput").ap()
    K.cid_in = dt("cid_in", [1, 1], U32, kind="ExternalInput").ap()
    K.invsel_in = dt("invsel_in", [16, 1], U32, kind="ExternalInput").ap()
    K.dmask_in = dt("dmask_in", [128, NH * 128], F32, kind="ExternalInput").ap()
    K.qdec_in = dt("qdec_in", [128, NH], F32, kind="ExternalInput").ap()
    K.kdec_in = dt("kdec_in", [128, NH], F32, kind="ExternalInput").ap()
    K.out = dt("out", [S, D], F32, kind="ExternalOutput").ap()

    K.xres = dt("xres", [S, D], F32).ap()
    K.lc = dt("lc", [S, D], BF16).ap()
    K.gag = dt("gag", [NTOK, D], BF16).ap()
    K.cnt1 = dt("cnt1", [1, E], F32).ap()
    K.cnt8 = dt("cnt8", [NCORES, E], F32).ap()
    K.invd = dt("invd", [NTOK, 1], F32).ap()
    K.invr = dt("invr", [NTOK, 1], F32).ap()
    K.buftm = dt("buftm", [2 * CAP, D], BF16).ap()
    K.yd = dt("yd", [2 * CAP, D], BF16).ap()
    K.yag = dt("yag", [NTOK, D], BF16).ap()
    K.siltm = dt("siltm", [S, D], BF16).ap()
    K.rettm = dt("rettm", [S, D], BF16).ap()

    sb = lambda name, shape, dtp: ctx.enter_context(nc.sbuf_tensor(name, shape, dtp))

    K.iota16 = sb("iota16", [128, E], F32)
    K.upper_bf = sb("upper_bf", [128, 128], BF16)
    K.ident = sb("ident", [128, 128], F32)
    K.onesc_bf = sb("onesc_bf", [128, 1], BF16)
    K.ones1_bf = sb("ones1_bf", [1, 128], BF16)
    K.ones1_f = sb("ones1_f", [1, 128], F32)
    K.dmaskT = sb("dmaskT", [128, NH * 128], BF16)
    K.qdec = sb("qdec", [128, NH], F32)
    K.kdec = sb("kdec", [128, NH], F32)
    K.gcrow = sb("gcrow", [128, NH], F32)
    K.iotacol1 = sb("iotacol1", [128, 1], F32)
    K.iotacol2 = sb("iotacol2", [128, 1], F32)
    K.wg_bf = sb("wg_bf", [128, DC * E], BF16)
    K.bg_bf = sb("bg_bf", [1, E], BF16)
    K.wq_bf = sb("wq_bf", [128, NH * DH], BF16)
    K.wk_bf = sb("wk_bf", [128, NH * DH], BF16)
    K.wv_bf = sb("wv_bf", [128, NH * DH], BF16)
    K.gnw_col = sb("gnw_col", [128, DC], F32)
    K.gnb_bf = sb("gnb_bf", [128, DC], BF16)
    K.b1c = sb("b1c", [128, 2 * HC], F32)
    K.b2r_bf = sb("b2r_bf", [1, 2 * D], BF16)
    K.tokp1 = sb("tokp1", [128, NT], F32)
    K.coresel = sb("coresel", [NCORES, 1], F32)
    K.cid_sb = sb("cid_sb", [1, 1], U32)
    K.invsel = sb("invsel", [16, 1], U32)
    K.z128 = sb("z128", [128, 128], F32)
    K.gapt = sb("gapt", [128, 1], F32)
    K.gapt2 = sb("gapt2", [128, 1], F32)
    K.warm1 = sb("warm1", [128, 512], F32)
    K.warm2 = sb("warm2", [128, 512], F32)
    K.scr_i = sb("scr_i", [128, 128], F32)
    K.scr_m = sb("scr_m", [128, 128], F32)

    K.lg = sb("lg", [128, E], F32)
    K.lgall = sb("lgall", [128, NT * E], F32)
    K.negm = sb("negm", [128, NT], F32)
    K.esum = sb("esum", [128, NT], F32)
    K.eall = sb("eall", [128, NT * E], F32)
    K.eall2 = sb("eall2", [128, NT * E], F32)
    K.eo = sb("eo", [128, E], F32)
    K.gates = sb("gates", [128, NT], F32)
    K.idxf = sb("idxf", [128, NT], F32)
    K.iotamb = sb("iotamb", [128, E], F32)
    K.eqs = sb("eqs", [128, E], F32)
    K.ohbf = sb("ohbf", [128, NT * E], BF16)
    K.ohf = sb("ohf", [128, NT * E], F32)
    K.posg = sb("posg", [128, E], F32)
    K.poscol = sb("poscol", [128, 1], F32)
    K.dropf = sb("dropf", [128, 1], F32)
    K.slotf = sb("slotf", [128, NT], F32)
    K.slotu = sb("slotu", [128, NT], U32)
    K.gate_eff = sb("gate_eff", [128, NT], F32)
    K.base = sb("base", [1, E], F32)
    K.cnt_sb = sb("cnt_sb", [1, E], F32)
    K.cnt8_sb = sb("cnt8_sb", [NCORES, E], F32)
    K.inv16 = sb("inv16", [16, 128], F32)
    K.invT = sb("invT", [128, NT], F32)
    K.negc = sb("negc", [128, NT], F32)
    K.gidxu = sb("gidxu", [128, NT], U32)
    K.r0_bf = sb("r0_bf", [1, D], BF16)

    K.xt = [sb(f"xt{i}", [128, D], F32) for i in range(2)]
    K.lnt = [sb(f"lnt{i}", [128, D], BF16) for i in range(2)]
    K.sqd = sb("sqd", [128, D], F32)
    K.srow = [sb(f"srow{i}", [128, 1], F32) for i in range(2)]
    K.ssq = [sb(f"ssq{i}", [128, 1], F32) for i in range(2)]
    K.mu = [sb(f"mu{i}", [128, 1], F32) for i in range(2)]
    K.t1 = [sb(f"t1_{i}", [128, 1], F32) for i in range(2)]
    K.t2 = [sb(f"t2_{i}", [128, 1], F32) for i in range(2)]
    K.rstd = [sb(f"rstd{i}", [128, 1], F32) for i in range(2)]
    K.bufstage = [sb(f"bufstage{i}", [128, D], BF16) for i in range(2)]
    K.rt = K.lnt
    K.yt = K.bufstage

    K.scrA = sb("scrA", [128, DC * S], BF16)
    K.scrB = sb("scrB", [128, HC * 512], BF16)
    K.scrC = sb("scrC", [128, HC * D], BF16)
    K.w1s = [sb(f"w1s{i}", [128, DC * 512], BF16) for i in range(2)]

    K.abar = sb("abar", [128, 512], BF16)
    K.S_sb = sb("S_sb", [128, 512], BF16)
    K.rets = K.sqd[:, 0:512]
    K.rsq = K.sqd[:, 512:1024]
    K.gnsum = sb("gnsum", [128, 128], F32)
    K.gnsq = sb("gnsq", [128, 128], F32)
    K.gnmu = sb("gnmu", [128, 128], F32)
    K.gnrs = sb("gnrs", [128, 128], F32)
    K.retgn = sb("retgn", [128, 512], BF16)
    K.muall = sb("muall", [128, NT], F32)
    K.vrall = sb("vrall", [128, NT], F32)
    K.rstall = sb("rstall", [128, NT], F32)
    K.nw1 = sb("nw1", [128, 128], F32)

    K.pA = ctx.enter_context(nc.psum_tensor("pA", [128, 1024], F32))
    K.pB = ctx.enter_context(nc.psum_tensor("pB", [128, 1024], F32))
    K.pC = ctx.enter_context(nc.psum_tensor("pC", [128, 1024], F32))
    K.pD = ctx.enter_context(nc.psum_tensor("pD", [128, 1024], F32))

    K.block = BlockWrap(ctx.enter_context(nc.Block()))

    K.sL = Sem(ctx, nc, "sL")
    K.sG = Sem(ctx, nc, "sG")
    K.sCC = Sem(ctx, nc, "sCC")
    K.sT = Sem(ctx, nc, "sT")
    K.sV = Sem(ctx, nc, "sV")
    K.sS = Sem(ctx, nc, "sS")
    K.sGc = Sem(ctx, nc, "sGc")
    K.sSc = Sem(ctx, nc, "sSc")

    K.dbg = {}
    for name in _DBG:
        shp = {"lc": [S, D], "gag": [NTOK, D], "invr": [NTOK, 1], "buftm": [2 * CAP, D],
               "yd": [2 * CAP, D], "x1": [S, D], "x2": [S, D], "slotu": [128, NT],
               "gates": [128, NT], "gidxu": [128, NT], "idxf": [128, NT],
               "siltm": [S, D], "rettm": [S, D], "geff": [128, NT]}[name]
        dtp = {"lc": BF16, "gag": BF16, "buftm": BF16, "yd": BF16, "slotu": U32,
               "gidxu": U32, "siltm": BF16, "rettm": BF16}.get(name, F32)
        K.dbg[name] = dt("dbg_" + name, shp, dtp, kind="ExternalOutput").ap()

    emit_setup(K)
    emit_ffn(K, tag="f1", src=K.x_in, dst=K.xres)
    emit_attn(K)
    emit_ffn(K, tag="f2", src=K.xres, dst=K.out, preload_exp=True)
    emit_finale(K)
    return K


def emit_setup(K):
    blk = K.block
    sL, sG, sV, sS, sGc = K.sL, K.sG, K.sV, K.sS, K.sGc

    @blk.sync
    def _(sy):
        sL.dma(sy.dma_start(out=K.tokp1[:], in_=K.tokp1_in))
        sL.dma(sy.dma_start(out=K.coresel[:], in_=K.coresel_in))
        sL.dma(sy.dma_start(out=K.cid_sb[:], in_=K.cid_in))
        sL.dma(sy.dma_start(out=K.invsel[:], in_=K.invsel_in))
        sL.dma(sy.dma_start(out=K.gnw_col[:], in_=K.gnw_in))
        sL.dma(sy.dma_start(out=K.b1c[:], in_=K.myb1_in))

    @blk.gpsimd
    def _(g):
        K.bnd = g.to_reg(NTOK - 1)
        sG.dma(g.dma_start(out=K.gnb_bf[:], in_=K.gnb_in))
        sG.dma(g.dma_start(out=K.b2r_bf[:], in_=K.myb2_in))
        sG.dma(g.dma_start(out=K.bg_bf[:], in_=K.bg_in))
        sG.dma(g.dma_start(
            out=K.wg_bf[:].rearrange("p (k e) -> p k e", k=DC),
            in_=K.wg_in.rearrange("(k p) e -> p k e", p=128)))
        for w_sb, w_in in ((K.wq_bf, K.wq_in), (K.wk_bf, K.wk_in), (K.wv_bf, K.wv_in)):
            sG.dma(g.dma_start(
                out=w_sb[:].rearrange("p (h j) -> p h j", h=NH),
                in_=w_in.rearrange("h p j -> p h j")))
        g.iota(K.iota16[:], pattern=[[1, E]], channel_multiplier=0,
               allow_small_or_imprecise_dtypes=True)
        g.iota(K.scr_i[:], pattern=[[1, 128]], channel_multiplier=-1,
               allow_small_or_imprecise_dtypes=True)
        g.iota(K.iotacol1[:], pattern=[[0, 1]], channel_multiplier=1, base=1,
               allow_small_or_imprecise_dtypes=True)
        g.iota(K.iotacol2[:], pattern=[[0, 1]], channel_multiplier=-1, base=127,
               allow_small_or_imprecise_dtypes=True)
        sGc.inc(g.last)

    @blk.vector
    def _(v):
        v.memset(K.onesc_bf[:], 1.0)
        v.memset(K.ones1_bf[:], 1.0)
        v.memset(K.ones1_f[:], 1.0)
        v.memset(K.z128[:], 0.0)
        v.memset(K.lnt[0][:], 0.0)
        v.memset(K.lnt[1][:], 0.0)
        v.memset(K.ident[:], 1.0)
        v.memset(K.scr_m[:], 1.0)
        for h in range(NH):
            v.memset(K.gcrow[:, h:h + 1], GC[h])
        sV.inc(v.last)

    @blk.gpsimd
    def _(g):
        g.wait_ge(sV.h, sV.n)
        g.affine_select(K.ident[:], K.ident[:], pattern=[[1, 128]],
                        compare_op=ALU.is_equal, fill=0.0, channel_multiplier=-1)
        g.affine_select(K.scr_m[:], K.scr_m[:], pattern=[[1, 128]],
                        compare_op=ALU.is_gt, fill=0.0, channel_multiplier=-1)
        sGc.inc(g.last)

    @blk.vector
    def _(v):
        v.wait_ge(sGc.h, sGc.n)
        v.tensor_copy(K.upper_bf[:], K.scr_m[:])
        v.tensor_scalar_sub(K.iotamb[:], K.iota16[:], 1000.0)
        sV.inc(v.last)

    @blk.gpsimd
    def _(g):
        sG.dma(g.dma_start(out=K.dmaskT[:], in_=K.dmask_in))

    @blk.sync
    def _(sy):
        sL.dma(sy.dma_start(out=K.qdec[:], in_=K.qdec_in))
        sL.dma(sy.dma_start(out=K.kdec[:], in_=K.kdec_in))

    @blk.vector
    def _(v):
        v.memset(K.warm1[:], 0.25)
        sV.inc(v.last)

    @blk.scalar
    def _(s):
        s.wait_ge(sV.h, sV.n)
        act_preload(s, AF.Exp, K)
        sS.inc(s.last)

    K.setup = {"g": K.sG.n, "s": K.sS.n, "v": K.sV.n, "gc": K.sGc.n, "l": K.sL.n}


def emit_rsqrt(v, K, dst, src, n, warr):
    I32 = mybir.dt.int32
    h = n // 2
    parts = [(dst[:, 0:h], src[:, 0:h], warr[:, 0:h]),
             (dst[:, h:n], src[:, h:n], warr[:, h:n])]
    for d, s_, w in parts:
        v.tensor_single_scalar(d.bitcast(I32), s_.bitcast(I32), 1,
                               op=ALU.arith_shift_right)
    for d, s_, w in parts:
        v.tensor_single_scalar(d.bitcast(I32), d.bitcast(I32), 0x5F3759DF,
                               op=ALU.subtract)
    for d, s_, w in parts:
        v.tensor_single_scalar(d.bitcast(I32), d.bitcast(I32), -1, op=ALU.mult)
    for _ in range(3):
        for d, s_, w in parts:
            v.tensor_tensor(w, d, d, op=ALU.mult)
        for d, s_, w in parts:
            v.tensor_tensor(w, w, s_, op=ALU.mult)
        for d, s_, w in parts:
            v.tensor_scalar(w, w, -0.5, 1.5, op0=ALU.mult, op1=ALU.add)
        for d, s_, w in parts:
            v.tensor_tensor(d, d, w, op=ALU.mult)


def emit_ln(K, src, order=None):
    blk = K.block
    sL, sV, sS = K.sL, K.sV, K.sS
    src_ready = sL.n
    src_ready_sc = K.sSc.n
    vdone = [None, None]
    sdone = [None, None]
    ssq_cons = [None, None]
    for _j, i in enumerate(order or range(NT)):
        b = _j % 2

        @blk.sync
        def _(sy, i=i, b=b):
            sy.wait_ge(sL.h, src_ready)
            sy.wait_ge(K.sSc.h, src_ready_sc)
            if vdone[b] is not None:
                sy.wait_ge(sV.h, vdone[b])
                sy.wait_ge(sS.h, sdone[b])
            sL.dma(sy.dma_start(out=K.xt[b][:], in_=src[i * 128:(i + 1) * 128, :]))

        ld_n = sL.n

        @blk.vector
        def _(v, b=b):
            v.wait_ge(sL.h, ld_n)
            v.reduce_sum(K.srow[b][:], K.xt[b][:], axis=AX.X)
            sV.inc(v.last)

        @blk.scalar
        def _(s, b=b):
            s.wait_ge(sL.h, ld_n)
            if ssq_cons[b] is not None:
                s.wait_ge(sV.h, ssq_cons[b])
            s.activation(K.sqd[:], K.xt[b][:], AF.Square, accum_out=K.ssq[b][:])
            sS.inc(s.last)

        sdone[b] = sS.n

        @blk.vector
        def _(v, b=b, i=i):
            v.wait_ge(sS.h, sS.n)
            v.tensor_scalar(K.t2[b][:], K.ssq[b][:], 1.0 / D, EPS,
                            op0=ALU.mult, op1=ALU.add)
            v.tensor_scalar_mul(K.muall[:, i:i + 1], K.srow[b][:], 1.0 / D)
            vgap(v, K)
            v.tensor_tensor(K.t1[b][:], K.muall[:, i:i + 1], K.muall[:, i:i + 1],
                            op=ALU.mult)
            vgap(v, K)
            v.tensor_tensor(K.vrall[:, i:i + 1], K.t2[b][:], K.t1[b][:],
                            op=ALU.subtract)
            sV.inc(v.last)
        ssq_cons[b] = sV.n
        vdone[b] = sV.n

    @blk.vector
    def _(v):
        emit_rsqrt(v, K, K.rstall[:], K.vrall[:], NT, K.nw1[:, 0:NT])
        sV.inc(v.last)

    lnt_free = [None, None]
    xfree = [sdone[0], sdone[1]]
    for _j, i in enumerate(order or range(NT)):
        b = _j % 2

        @blk.sync
        def _(sy, i=i, b=b):
            sy.wait_ge(K.sSc.h, src_ready_sc)
            if vdone[b] is not None:
                sy.wait_ge(sV.h, vdone[b])
            sy.wait_ge(sS.h, xfree[b])
            sL.dma(sy.dma_start(out=K.xt[b][:], in_=src[i * 128:(i + 1) * 128, :]))

        ld_n = sL.n

        @blk.vector
        def _(v, b=b, i=i):
            v.wait_ge(sL.h, ld_n)
            if lnt_free[b] is not None:
                v.wait_ge(sL.h, lnt_free[b])
            v.tensor_scalar(K.lnt[b][:], K.xt[b][:], K.muall[:, i:i + 1],
                            K.rstall[:, i:i + 1], op0=ALU.subtract, op1=ALU.mult)
            sV.inc(v.last)

        vdone[b] = sV.n

        @blk.sync
        def _(sy, i=i, b=b):
            sy.wait_ge(sV.h, sV.n)
            sL.dma(sy.dma_start(out=K.lc[i * 128:(i + 1) * 128, :], in_=K.lnt[b][:]))

        lnt_free[b] = sL.n


def emit_ffn(K, tag, src, dst, preload_exp=False):
    blk = K.block
    sL, sG, sCC, sT, sV, sS = K.sL, K.sG, K.sCC, K.sT, K.sV, K.sS

    lnfm = K.scrA
    buf_fm = K.scrA
    hfm = K.scrB
    w2e = K.scrC[:, 0:HC * D]

    if _CUT == 'cut0':
        emit_dbg(K, tag)
        return
    emit_ln(K, src)
    lc_done = sL.n
    if _CUT == 'ln0':
        emit_dbg(K, tag)
        return

    @blk.gpsimd
    def _(g):
        g.wait_ge(sL.h, lc_done)
        sCC.inc(g.collective_compute(
            "AllGather", ALU.bypass, replica_groups=[list(range(NCORES))],
            ins=[K.lc.opt()], outs=[K.gag.opt()]))
    ag_lc = sCC.n
    if _CUT == 'ag':
        emit_dbg(K, tag)
        return

    @blk.sync
    def _(sy):
        sL.dma(sy.dma_start(
            out=K.invd.rearrange("(a b) o -> a (b o)", a=128), in_=K.z128[:]))
    inv_zeroed = sL.n
    if _CUT == 'zero':
        emit_dbg(K, tag)
        return

    @blk.sync
    def _(sy):
        sy.wait_ge(sL.h, lc_done)
        for j in range(DC):
            sL.dma(sy.dma_start(out=lnfm[:, j * S:(j + 1) * S],
                                in_=K.lc[:, j * 128:(j + 1) * 128], transpose=True))
    lnfm_done = sL.n
    if _CUT == 'ln':
        emit_dbg(K, tag)
        return

    for tc in range(NT):
        @blk.tensor
        def _(t, tc=tc):
            t.wait_ge(sL.h, lnfm_done)
            if tc == 0:
                t.wait_ge(sG.h, K.setup["g"])
                t.wait_ge(K.sGc.h, K.setup["gc"])
            if tc > 0:
                t.wait_ge(sV.h, sV.n)
            for k in range(DC):
                t.matmul(K.pA[:, 0:E], lnfm[:, k * S + tc * 128:k * S + tc * 128 + 128],
                         K.wg_bf[:, k * E:(k + 1) * E], start=(k == 0), stop=False)
            sT.inc(t.matmul(K.pA[:, 0:E], K.ones1_bf[:], K.bg_bf[:],
                            start=False, stop=True))

        @blk.vector
        def _(v, tc=tc):
            v.wait_ge(sT.h, sT.n)
            v.tensor_copy(K.lgall[:, tc * E:(tc + 1) * E], K.pA[:, 0:E])
            vgap(v, K)
            lg = K.lgall[:, tc * E:(tc + 1) * E]
            v.reduce_max(K.mu[0][:], lg, axis=AX.X)
            vgap(v, K)
            v.tensor_tensor(K.eqs[:], lg,
                            K.mu[0][:, 0:1].broadcast_to((128, E)), op=ALU.is_equal)
            vgap(v, K)
            v.tensor_tensor(K.eqs[:], K.eqs[:], K.iotamb[:], op=ALU.mult)
            vgap(v, K)
            v.tensor_scalar_add(K.eqs[:], K.eqs[:], 1000.0)
            vgap(v, K)
            v.tensor_reduce(K.idxf[:, tc:tc + 1], K.eqs[:], axis=AX.X, op=ALU.min)
            vgap(v, K)
            v.tensor_tensor(K.ohbf[:, tc * E:(tc + 1) * E], K.iota16[:],
                            K.idxf[:, tc:tc + 1].broadcast_to((128, E)),
                            op=ALU.is_equal)
            v.tensor_tensor(K.ohf[:, tc * E:(tc + 1) * E], K.iota16[:],
                            K.idxf[:, tc:tc + 1].broadcast_to((128, E)),
                            op=ALU.is_equal)
            sV.inc(v.last)

        @blk.tensor
        def _(t, tc=tc):
            t.wait_ge(sV.h, sV.n)
            sT.inc(t.matmul(K.pB[0:1, 0:E], K.onesc_bf[:],
                            K.ohbf[:, tc * E:(tc + 1) * E],
                            start=(tc == 0), stop=(tc == NT - 1)))

    @blk.vector
    def _(v):
        v.tensor_reduce(K.negm[:].rearrange("p (a o) -> p a o", o=1),
                        K.lgall[:].rearrange("p (a e) -> p a e", a=NT),
                        axis=AX.X, op=ALU.max)
        vgap(v, K)
        v.tensor_tensor(K.eall[:].rearrange("p (a e) -> p a e", a=NT),
                        K.lgall[:].rearrange("p (a e) -> p a e", a=NT),
                        K.negm[:].rearrange("p (a o) -> p a o", o=1)
                            .broadcast_to((128, NT, E)),
                        op=ALU.subtract)
        sV.inc(v.last)

    @blk.scalar
    def _(s):
        if preload_exp:
            act_preload(s, AF.Exp, K)
        s.wait_ge(sV.h, sV.n)
        s.activation(K.eall2[:], K.eall[:], AF.Exp)
        sS.inc(s.last)

    @blk.vector
    def _(v):
        v.wait_ge(sS.h, sS.n)
        v.tensor_reduce(K.esum[:].rearrange("p (a o) -> p a o", o=1),
                        K.eall2[:].rearrange("p (a e) -> p a e", a=NT),
                        axis=AX.X, op=ALU.add)
        vgap(v, K)
        v.reciprocal(K.gates[:], K.esum[:])
        sV.inc(v.last)

    if _CUT == 'router':
        emit_dbg(K, tag)
        return
    @blk.vector
    def _(v):
        v.wait_ge(sT.h, sT.n)
        v.tensor_copy(K.cnt_sb[:], K.pB[0:1, 0:E])
        sV.inc(v.last)

    @blk.sync
    def _(sy):
        sy.wait_ge(sV.h, sV.n)
        sL.dma(sy.dma_start(out=K.cnt1, in_=K.cnt_sb[:]))
    cnt_st = sL.n

    @blk.gpsimd
    def _(g):
        g.wait_ge(sL.h, cnt_st)
        sCC.inc(g.collective_compute(
            "AllGather", ALU.bypass, replica_groups=[list(range(NCORES))],
            ins=[K.cnt1.opt()], outs=[K.cnt8.opt()]))

    @blk.sync
    def _(sy):
        sy.wait_ge(sCC.h, sCC.n)
        sL.dma(sy.dma_start(out=K.cnt8_sb[:], in_=K.cnt8))
    cnt8_ld = sL.n

    @blk.tensor
    def _(t):
        t.wait_ge(sL.h, cnt8_ld)
        sT.inc(t.matmul(K.pB[0:1, 0:E], K.coresel[:], K.cnt8_sb[:],
                        start=True, stop=True))

    @blk.vector
    def _(v):
        v.wait_ge(sT.h, sT.n)
        v.tensor_copy(K.base[:], K.pB[0:1, 0:E])
        sV.inc(v.last)

    for tc in range(NT):
        @blk.tensor
        def _(t, tc=tc):
            t.wait_ge(sV.h, sV.n)
            t.matmul(K.pA[:, 0:E], K.upper_bf[:], K.ohbf[:, tc * E:(tc + 1) * E],
                     start=True, stop=True)
            t.matmul(K.pC[:, 0:E], K.ones1_f[:], K.base[:], start=True, stop=True)
            sT.inc(t.matmul(K.pB[0:1, 0:E], K.onesc_bf[:],
                            K.ohbf[:, tc * E:(tc + 1) * E], start=True, stop=True))

        @blk.vector
        def _(v, tc=tc):
            v.wait_ge(sT.h, sT.n)
            v.tensor_copy(K.posg[:], K.pA[:, 0:E])
            v.tensor_tensor(K.base[:], K.base[:], K.pB[0:1, 0:E], op=ALU.add)
            v.tensor_tensor(K.posg[:], K.posg[:], K.pC[:, 0:E], op=ALU.add)
            vgap(v, K)
            v.tensor_tensor(K.posg[:], K.posg[:], K.ohf[:, tc * E:(tc + 1) * E],
                            op=ALU.mult)
            vgap(v, K)
            v.reduce_sum(K.poscol[:], K.posg[:], axis=AX.X)
            vgap(v, K)
            v.tensor_single_scalar(K.dropf[:], K.poscol[:], float(CAP), op=ALU.is_ge)
            v.scalar_tensor_tensor(K.slotf[:, tc:tc + 1], K.idxf[:, tc:tc + 1],
                                   float(CAP), K.poscol[:],
                                   op0=ALU.mult, op1=ALU.add)
            v.tensor_scalar(K.t1[0][:], K.dropf[:], -0.5, 0.5, op0=ALU.mult, op1=ALU.add)
            v.scalar_tensor_tensor(K.slotf[:, tc:tc + 1], K.dropf[:], BIGF,
                                   K.slotf[:, tc:tc + 1], op0=ALU.mult, op1=ALU.add)
            v.tensor_tensor(K.gate_eff[:, tc:tc + 1], K.t1[0][:],
                            K.gates[:, tc:tc + 1], op=ALU.mult)
            v.tensor_copy(K.slotu[:, tc:tc + 1], K.slotf[:, tc:tc + 1])
            sV.inc(v.last)
        slot_n = sV.n

        @blk.gpsimd
        def _(g, tc=tc):
            g.wait_ge(sV.h, slot_n)
            g.wait_ge(sL.h, inv_zeroed)
            sG.dma(g.indirect_dma_start(
                out=K.invd, out_offset=IndirectOffsetOnAxis(K.slotu[:, tc:tc + 1], 0),
                in_=K.tokp1[:, tc:tc + 1], in_offset=None,
                bounds_check=K.bnd, oob_is_err=False))

    if _CUT == 'prefix':
        emit_dbg(K, tag)
        return
    @blk.gpsimd
    def _(g):
        g.wait_ge(sG.h, sG.n)
        sCC.inc(g.collective_compute(
            "AllReduce", ALU.add, replica_groups=[list(range(NCORES))],
            ins=[K.invd.opt()], outs=[K.invr.opt()]))

    @blk.gpsimd
    def _(g):
        g.wait_ge(sCC.h, sCC.n)
        sG.dma(g.indirect_dma_start(
            out=K.inv16[:], out_offset=None,
            in_=K.invr.rearrange("(a b) o -> a (b o)", a=128),
            in_offset=IndirectOffsetOnAxis(K.invsel[:, 0:1], 0),
            bounds_check=K.bnd, oob_is_err=False))
    inv_ld = sG.n

    @blk.tensor
    def _(t):
        t.wait_ge(sG.h, inv_ld)
        t.wait_ge(K.sGc.h, K.setup["gc"])
        t.wait_ge(sV.h, sV.n)
        sT.inc(t.transpose(K.pA[:, 0:16], K.inv16[:], K.ident[0:16, 0:16]))

    @blk.vector
    def _(v):
        v.wait_ge(sT.h, sT.n)
        v.tensor_copy(K.invT[:], K.pA[:, 0:16])
        vgap(v, K)
        v.tensor_scalar_sub(K.invT[:], K.invT[:], 1.0)
        vgap(v, K)
        v.tensor_single_scalar(K.negc[:], K.invT[:], 0.0, op=ALU.is_lt)
        vgap(v, K)
        v.scalar_tensor_tensor(K.invT[:], K.negc[:], 2.0e9, K.invT[:],
                               op0=ALU.mult, op1=ALU.add)
        vgap(v, K)
        v.tensor_copy(K.gidxu[:], K.invT[:])
        sV.inc(v.last)
    gidx_done = sV.n
    if _CUT == 'inv':
        emit_dbg(K, tag)
        return

    stage_free = [None, None]
    for sbk in range(NT):
        b = sbk % 2

        @blk.gpsimd
        def _(g, sbk=sbk, b=b):
            g.wait_ge(sCC.h, ag_lc)
            g.wait_ge(sV.h, gidx_done)
            if stage_free[b] is not None:
                g.wait_ge(sL.h, stage_free[b])
            sG.dma(g.indirect_dma_start(
                out=K.bufstage[b][:], out_offset=None,
                in_=K.gag, in_offset=IndirectOffsetOnAxis(K.gidxu[:, sbk:sbk + 1], 0),
                bounds_check=K.bnd, oob_is_err=False))
        g_n = sG.n

        @blk.sync
        def _(sy, sbk=sbk, b=b):
            sy.wait_ge(sG.h, g_n)
            sL.dma(sy.dma_start(out=K.buftm[sbk * 128:(sbk + 1) * 128, :],
                                in_=K.bufstage[b][:]))
        stage_free[b] = sL.n
    buftm_done = sL.n
    if _CUT == 'gather':
        emit_dbg(K, tag)
        return

    @blk.sync
    def _(sy):
        sy.wait_ge(sL.h, buftm_done)
        for j in range(DC):
            sL.dma(sy.dma_start(out=buf_fm[:, j * 2 * CAP:(j + 1) * 2 * CAP],
                                in_=K.buftm[:, j * 128:(j + 1) * 128], transpose=True))
    buffm_done = sL.n
    if _CUT == 'xbar':
        emit_dbg(K, tag)
        return

    psum_l1 = [(K.pA, 0), (K.pA, 512), (K.pB, 0), (K.pB, 512)]
    psum_free = [sS.n] * 4
    w2_loaded = [None, None]
    w2_free = None
    w1_free = [None, None]
    yt_free = [None, None]
    for el in range(2):
        @blk.gpsimd
        def _(g, el=el):
            if el == 1 and w2_free is not None:
                g.wait_ge(sT.h, w2_free)
            for piece in range(4):
                sG.dma(g.dma_start(
                    out=w2e[:, piece * 8 * D:(piece + 1) * 8 * D]
                        .rearrange("p (hc d) -> p hc d", hc=8),
                    in_=K.myw2_in[el]
                        .rearrange("(hc p) d -> p hc d", p=128)[:, piece * 8:(piece + 1) * 8, :]))
        w2_loaded[el] = sG.n
        if _CUT == 'w2l' and el == 0:
            emit_dbg(K, tag)
            return

        for half in range(2):
            for hs in range(8):
                wbuf = hs % 2

                @blk.gpsimd
                def _(g, el=el, hs=hs, wbuf=wbuf):
                    if w1_free[wbuf] is not None:
                        g.wait_ge(sS.h, w1_free[wbuf])
                    sG.dma(g.dma_start(
                        out=K.w1s[wbuf][:].rearrange("p (k h) -> p k h", k=DC),
                        in_=K.myw1_in[el]
                            .rearrange("(k p) h -> p k h", p=128)[:, :, hs * 512:(hs + 1) * 512]))
                w1_ld = sG.n

                for hc4 in range(4):
                    ps, off = psum_l1[hc4]

                    @blk.tensor
                    def _(t, el=el, half=half, hs=hs, hc4=hc4, wbuf=wbuf, ps=ps, off=off,
                          w1_ld=w1_ld, pf=psum_free[hc4]):
                        t.wait_ge(sG.h, w1_ld)
                        t.wait_ge(sL.h, buffm_done)
                        t.wait_ge(sS.h, pf)
                        for k in range(DC):
                            t.matmul(
                                ps[:, off:off + 512],
                                K.w1s[wbuf][:, k * 512 + hc4 * 128:k * 512 + hc4 * 128 + 128],
                                buf_fm[:, k * 2 * CAP + el * CAP + half * 512:
                                       k * 2 * CAP + el * CAP + half * 512 + 512],
                                start=(k == 0), stop=(k == DC - 1))
                        sT.inc(t.last)

                    @blk.scalar
                    def _(s, el=el, hs=hs, hc4=hc4, ps=ps, off=off):
                        s.wait_ge(sT.h, sT.n)
                        hc = hs * 4 + hc4
                        s.activation(hfm[:, hc * 512:(hc + 1) * 512],
                                     ps[:, off:off + 512], AF.Relu,
                                     bias=K.b1c[:, el * HC + hc:el * HC + hc + 1])
                        sS.inc(s.last)

                    psum_free[hc4] = sS.n

                w1_free[wbuf] = sS.n
            l1_done = sS.n
            if _CUT == 'l1a' and el == 0 and half == 0:
                emit_dbg(K, tag)
                return

            for tc4 in range(4):
                b = tc4 % 2

                @blk.tensor
                def _(t, el=el, half=half, tc4=tc4, l1_done=l1_done):
                    t.wait_ge(sS.h, l1_done)
                    t.wait_ge(sG.h, w2_loaded[el])
                    t.wait_ge(sV.h, sV.n)
                    for dt2 in range(2):
                        for hc in range(HC):
                            t.matmul(
                                K.pC[:, dt2 * 512:dt2 * 512 + 512],
                                hfm[:, hc * 512 + tc4 * 128:hc * 512 + tc4 * 128 + 128],
                                w2e[:, hc * D + dt2 * 512:hc * D + dt2 * 512 + 512],
                                start=(hc == 0), stop=False)
                        t.matmul(K.pC[:, dt2 * 512:dt2 * 512 + 512], K.ones1_bf[:],
                                 K.b2r_bf[0:1, el * D + dt2 * 512:el * D + dt2 * 512 + 512],
                                 start=False, stop=True)
                    sT.inc(t.last)

                @blk.vector
                def _(v, tc4=tc4, b=b):
                    v.wait_ge(sT.h, sT.n)
                    if yt_free[b] is not None:
                        v.wait_ge(sL.h, yt_free[b])
                    v.tensor_copy(K.yt[b][:], K.pC[:])
                    sV.inc(v.last)

                @blk.sync
                def _(sy, el=el, half=half, tc4=tc4, b=b):
                    sy.wait_ge(sV.h, sV.n)
                    row = el * CAP + half * 512 + tc4 * 128
                    sL.dma(sy.dma_start(out=K.yd[row:row + 128, :], in_=K.yt[b][:]))
                yt_free[b] = sL.n
        w2_free = sT.n
        if _CUT == 'el0' and el == 0:
            emit_dbg(K, tag)
            return
    yd_done = sL.n
    if _CUT == 'exp':
        emit_dbg(K, tag)
        return

    @blk.gpsimd
    def _(g):
        g.wait_ge(sL.h, yd_done)
        sCC.inc(g.collective_compute(
            "AllGather", ALU.bypass, replica_groups=[list(range(NCORES))],
            ins=[K.yd.opt()], outs=[K.yag.opt()]))
    ag_y = sCC.n

    rt_free = [None, None]
    xn_free = [None, None]
    src_ready = sL.n
    for tc in range(NT):
        b = tc % 2

        @blk.gpsimd
        def _(g, tc=tc, b=b):
            g.wait_ge(sCC.h, ag_y)
            if rt_free[b] is not None:
                g.wait_ge(sV.h, rt_free[b])
            sG.dma(g.indirect_dma_start(
                out=K.rt[b][:], out_offset=None,
                in_=K.yag, in_offset=IndirectOffsetOnAxis(K.slotu[:, tc:tc + 1], 0),
                bounds_check=K.bnd, oob_is_err=False))
        g_n = sG.n

        @blk.sync
        def _(sy, tc=tc, b=b):
            sy.wait_ge(sL.h, src_ready)
            if xn_free[b] is not None:
                sy.wait_ge(K.sSc.h, xn_free[b])
            sL.dma(sy.dma_start(out=K.xt[b][:], in_=src[tc * 128:(tc + 1) * 128, :]))
        l_n = sL.n

        @blk.vector
        def _(v, tc=tc, b=b, g_n=g_n, l_n=l_n):
            v.wait_ge(sG.h, g_n)
            v.wait_ge(sL.h, l_n)
            v.scalar_tensor_tensor(K.xt[b][:], K.rt[b][:],
                                   K.gate_eff[:, tc:tc + 1], K.xt[b][:],
                                   op0=ALU.mult, op1=ALU.add)
            sV.inc(v.last)
        rt_free[b] = sV.n

        @blk.scalar
        def _(sc, tc=tc, b=b):
            sc.wait_ge(sV.h, sV.n)
            K.sSc.dma(sc.dma_start(out=dst[tc * 128:(tc + 1) * 128, :], in_=K.xt[b][:]))
        xn_free[b] = K.sSc.n

    emit_dbg(K, tag)


def emit_dbg(K, tag):
    if not K.dbg:
        return
    blk = K.block
    sL, sV = K.sL, K.sV
    want = {"f1": ["lc", "gag", "invr", "buftm", "yd", "slotu", "gates", "gidxu",
                   "idxf", "x1", "geff"],
            "f2": [], "at": ["siltm", "rettm", "x2"]}[tag]

    @blk.sync
    def _(sy):
        sy.wait_ge(sL.h, sL.n)
        sy.wait_ge(sV.h, sV.n)
        sy.wait_ge(K.sCC.h, K.sCC.n)
        sy.wait_ge(K.sG.h, K.sG.n)
        sy.wait_ge(K.sS.h, K.sS.n)
        sy.wait_ge(K.sT.h, K.sT.n)
        sy.wait_ge(K.sSc.h, K.sSc.n)
        for name in want:
            if name not in K.dbg:
                continue
            if name in ("lc", "gag", "invr", "buftm", "yd", "siltm", "rettm"):
                sL.dma(sy.dma_start(out=K.dbg[name], in_=getattr(K, name)))
            elif name in ("x1", "x2"):
                sL.dma(sy.dma_start(out=K.dbg[name], in_=K.xres))
            elif name == "geff":
                sL.dma(sy.dma_start(out=K.dbg[name], in_=K.gate_eff[:]))
            else:
                sL.dma(sy.dma_start(out=K.dbg[name], in_=getattr(K, name)[:]))
        sy.wait_ge(sL.h, sL.n)


def emit_attn(K):
    if _CUT is not None:
        return
    blk = K.block
    sL, sG, sT, sV, sS = K.sL, K.sG, K.sT, K.sV, K.sS

    x_fm = K.scrA
    w1a = K.scrC[:, 0:DC * D]
    w2a = K.scrC[:, DC * D:2 * DC * D]
    slot3 = K.scrC[:, 2 * DC * D:3 * DC * D]
    vtm = K.scrC[:, 0:DC * D]
    kttm = slot3
    w2p = slot3
    qfm = K.scrB[:, 0:4 * S]
    kfm = K.scrB[:, 4 * S:8 * S]

    emit_ln(K, K.xres)
    lc_done = sL.n

    @blk.gpsimd
    def _(g):
        sG.dma(g.dma_start(out=w1a.rearrange("p (k j) -> p k j", k=DC),
                           in_=K.w1a_in.rearrange("(k p) j -> p k j", p=128)))
        sG.dma(g.dma_start(out=w2a.rearrange("p (k j) -> p k j", k=DC),
                           in_=K.w2a_in.rearrange("(k p) j -> p k j", p=128)))
    wa_done = sG.n

    @blk.tensor
    def _(t):
        t.wait_ge(sG.h, wa_done)
        t.wait_ge(sV.h, sV.n)
        for dt2 in range(2):
            for k in range(DC):
                t.matmul(K.pD[0:1, dt2 * 512:dt2 * 512 + 512],
                         K.gnb_bf[:, k:k + 1],
                         w2a[:, k * D + dt2 * 512:k * D + dt2 * 512 + 512],
                         start=(k == 0), stop=(k == DC - 1))
        sT.inc(t.last)

    @blk.vector
    def _(v):
        v.wait_ge(sT.h, sT.n)
        v.tensor_copy(K.r0_bf[:], K.pD[0:1, :])
        sV.inc(v.last)

    @blk.sync
    def _(sy):
        sy.wait_ge(sL.h, lc_done)
        for j in range(DC):
            sL.dma(sy.dma_start(out=x_fm[:, j * S:(j + 1) * S],
                                in_=K.lc[:, j * 128:(j + 1) * 128], transpose=True))
    xfm_done = sL.n

    silt_free = [None, None]
    for tc in range(NT):
        b = tc % 2

        @blk.tensor
        def _(t, tc=tc):
            t.wait_ge(sL.h, xfm_done)
            if tc == 0:
                t.wait_ge(sG.h, wa_done)
            t.wait_ge(sV.h, sV.n)
            for dt2 in range(2):
                for k in range(DC):
                    t.matmul(K.pA[:, dt2 * 512:dt2 * 512 + 512],
                             x_fm[:, k * S + tc * 128:k * S + tc * 128 + 128],
                             w1a[:, k * D + dt2 * 512:k * D + dt2 * 512 + 512],
                             start=(k == 0), stop=(k == DC - 1))
            sT.inc(t.last)

        @blk.scalar
        def _(s, tc=tc):
            if tc == 0:
                act_preload(s, AF.Sigmoid, K)
            s.wait_ge(sT.h, sT.n)
            s.wait_ge(sV.h, sV.n)
            s.activation(K.sqd[:], K.pA[:], AF.Sigmoid)
            sS.inc(s.last)

        @blk.vector
        def _(v, tc=tc, b=b):
            v.wait_ge(sS.h, sS.n)
            if silt_free[b] is not None:
                v.wait_ge(sL.h, silt_free[b])
            v.tensor_tensor(K.lnt[b][:], K.sqd[:], K.pA[:], op=ALU.mult)
            sV.inc(v.last)

        @blk.sync
        def _(sy, tc=tc, b=b):
            sy.wait_ge(sV.h, sV.n)
            sL.dma(sy.dma_start(out=K.siltm[tc * 128:(tc + 1) * 128, :], in_=K.lnt[b][:]))
        silt_free[b] = sL.n
    gated_mm_done = sT.n

    emit_retention(K, x_fm, qfm, kfm, vtm, kttm, gated_mm_done)
    retention_T_done = sT.n

    @blk.vector
    def _(v):
        v.wait_ge(sT.h, retention_T_done)
        for k in range(DC):
            v.tensor_scalar_mul(w2p[:, k * D:(k + 1) * D], w2a[:, k * D:(k + 1) * D],
                                K.gnw_col[:, k:k + 1])
        sV.inc(v.last)
    w2p_done = sV.n

    silu_fm = K.scrA
    retgn_fm = K.scrB

    @blk.sync
    def _(sy):
        sy.wait_ge(sL.h, sL.n)
        sy.wait_ge(sT.h, retention_T_done)
        for j in range(DC):
            sL.dma(sy.dma_start(out=silu_fm[:, j * S:(j + 1) * S],
                                in_=K.siltm[:, j * 128:(j + 1) * 128], transpose=True))
        for j in range(DC):
            sL.dma(sy.dma_start(out=retgn_fm[:, j * S:(j + 1) * S],
                                in_=K.rettm[:, j * 128:(j + 1) * 128], transpose=True))
    fm_done = sL.n

    xn_free = [None, None]
    src_ready = sL.n
    scr = K.sSc.n
    for tc in range(NT):
        b = tc % 2

        @blk.tensor
        def _(t, tc=tc):
            t.wait_ge(sL.h, fm_done)
            t.wait_ge(sV.h, sV.n)
            if tc == 0:
                t.wait_ge(K.sS.h, K.sS.n)
                t.wait_ge(sV.h, w2p_done)
            for dt2 in range(2):
                for k in range(DC):
                    t.matmul(K.pA[:, dt2 * 512:dt2 * 512 + 512],
                             silu_fm[:, k * S + tc * 128:k * S + tc * 128 + 128],
                             w2a[:, k * D + dt2 * 512:k * D + dt2 * 512 + 512],
                             start=(k == 0), stop=False)
                    t.matmul(K.pA[:, dt2 * 512:dt2 * 512 + 512],
                             retgn_fm[:, k * S + tc * 128:k * S + tc * 128 + 128],
                             w2p[:, k * D + dt2 * 512:k * D + dt2 * 512 + 512],
                             start=False, stop=False)
                t.matmul(K.pA[:, dt2 * 512:dt2 * 512 + 512], K.ones1_bf[:],
                         K.r0_bf[0:1, dt2 * 512:dt2 * 512 + 512],
                         start=False, stop=True)
            sT.inc(t.last)

        @blk.sync
        def _(sy, tc=tc, b=b, scr=scr):
            sy.wait_ge(sL.h, src_ready)
            sy.wait_ge(K.sSc.h, scr)
            if xn_free[b] is not None:
                sy.wait_ge(K.sSc.h, xn_free[b])
            sL.dma(sy.dma_start(out=K.xt[b][:], in_=K.xres[tc * 128:(tc + 1) * 128, :]))
        l_n = sL.n

        @blk.vector
        def _(v, tc=tc, b=b, l_n=l_n):
            v.wait_ge(sT.h, sT.n)
            v.wait_ge(sL.h, l_n)
            v.tensor_tensor(K.xt[b][:], K.pA[:], K.xt[b][:], op=ALU.add)
            sV.inc(v.last)

        @blk.scalar
        def _(sc, tc=tc, b=b):
            sc.wait_ge(sV.h, sV.n)
            K.sSc.dma(sc.dma_start(out=K.xres[tc * 128:(tc + 1) * 128, :], in_=K.xt[b][:]))
        xn_free[b] = K.sSc.n

    emit_dbg(K, "at")


def emit_retention(K, x_fm, qfm, kfm, vtm, kttm, gated_mm_done):
    blk = K.block
    sL, sT, sV, sS = K.sL, K.sT, K.sV, K.sS
    silu_evacs_done = sS.n

    for grp in range(2):
        heads = [grp * 4 + i for i in range(4)]

        for i4, h in enumerate(heads):
            for n in range(4):
                @blk.tensor
                def _(t, h=h, n=n, i4=i4):
                    if i4 == 0 and n == 0:
                        t.wait_ge(K.sG.h, K.setup["g"])
                        t.wait_ge(sS.h, K.setup["s"])
                    t.wait_ge(sV.h, sV.n)
                    t.wait_ge(sS.h, silu_evacs_done)
                    t.matmul(K.pA[:, 0:512], K.wq_bf[:, h * DH:(h + 1) * DH],
                             x_fm[:, h * S + n * 512:h * S + n * 512 + 512],
                             start=True, stop=True)
                    t.matmul(K.pA[:, 512:1024], K.wk_bf[:, h * DH:(h + 1) * DH],
                             x_fm[:, h * S + n * 512:h * S + n * 512 + 512],
                             start=True, stop=True)
                    sT.inc(t.last)

                @blk.vector
                def _(v, i4=i4, n=n):
                    v.wait_ge(sT.h, sT.n)
                    v.tensor_copy(qfm[:, i4 * S + n * 512:i4 * S + n * 512 + 512],
                                  K.pA[:, 0:512])
                    v.tensor_copy(kfm[:, i4 * S + n * 512:i4 * S + n * 512 + 512],
                                  K.pA[:, 512:1024])
                    sV.inc(v.last)

        for i4, h in enumerate(heads):
            for tc in range(NT):
                @blk.tensor
                def _(t, h=h, tc=tc):
                    t.wait_ge(sV.h, sV.n)
                    t.matmul(K.pB[:, 0:128],
                             x_fm[:, h * S + tc * 128:h * S + tc * 128 + 128],
                             K.wv_bf[:, h * DH:(h + 1) * DH], start=True, stop=True)
                    t.matmul(K.pB[:, 128:256],
                             x_fm[:, h * S + tc * 128:h * S + tc * 128 + 128],
                             K.wk_bf[:, h * DH:(h + 1) * DH], start=True, stop=True)
                    sT.inc(t.last)

                @blk.vector
                def _(v, h=h, i4=i4, tc=tc):
                    v.wait_ge(sT.h, sT.n)
                    v.tensor_copy(vtm[:, i4 * S + tc * 128:i4 * S + tc * 128 + 128],
                                  K.pB[:, 0:128])
                    v.tensor_scalar_mul(
                        kttm[:, i4 * S + tc * 128:i4 * S + tc * 128 + 128],
                        K.pB[:, 128:256], K.kdec[:, h:h + 1])
                    sV.inc(v.last)

        retgn_free = [None]

        @blk.vector
        def _(v):
            v.memset(K.S_sb[:], 0.0)
            sV.inc(v.last)

        for ci in range(NT):
            @blk.tensor
            def _(t, ci=ci, grp=grp):
                t.wait_ge(sV.h, sV.n)
                for i4 in range(4):
                    t.matmul(K.pA[:, i4 * 128:(i4 + 1) * 128],
                             kfm[:, i4 * S + ci * 128:i4 * S + ci * 128 + 128],
                             qfm[:, i4 * S + ci * 128:i4 * S + ci * 128 + 128],
                             start=True, stop=True)
                sT.inc(t.last)

            @blk.vector
            def _(v, ci=ci, grp=grp):
                v.wait_ge(sT.h, sT.n)
                v.tensor_tensor(K.abar[:], K.pA[:, 0:512],
                                K.dmaskT[:, grp * 512:(grp + 1) * 512], op=ALU.mult)
                sV.inc(v.last)
            abar_n = sV.n

            @blk.tensor
            def _(t, ci=ci, abar_n=abar_n):
                t.wait_ge(sV.h, abar_n)
                for i4 in range(4):
                    t.matmul(K.pB[:, i4 * 128:(i4 + 1) * 128],
                             K.abar[:, i4 * 128:(i4 + 1) * 128],
                             vtm[:, i4 * S + ci * 128:i4 * S + ci * 128 + 128],
                             start=True, stop=True)
                    t.matmul(K.pC[:, i4 * 128:(i4 + 1) * 128],
                             qfm[:, i4 * S + ci * 128:i4 * S + ci * 128 + 128],
                             K.S_sb[:, i4 * 128:(i4 + 1) * 128],
                             start=True, stop=True)
                    t.matmul(K.pD[:, i4 * 128:(i4 + 1) * 128],
                             kttm[:, i4 * S + ci * 128:i4 * S + ci * 128 + 128],
                             vtm[:, i4 * S + ci * 128:i4 * S + ci * 128 + 128],
                             start=True, stop=True)
                sT.inc(t.last)
            ret_mm = sT.n

            @blk.vector
            def _(v, ci=ci, grp=grp, ret_mm=ret_mm):
                v.wait_ge(sT.h, ret_mm)
                v.tensor_tensor(
                    K.rets.rearrange("p (a b) -> p a b", a=4),
                    K.pC[:, 0:512].rearrange("p (a b) -> p a b", a=4),
                    K.qdec[:, grp * 4:(grp + 1) * 4]
                        .rearrange("p (a o) -> p a o", o=1).broadcast_to((128, 4, 128)),
                    op=ALU.mult)
                v.tensor_tensor(
                    K.rsq.rearrange("p (a b) -> p a b", a=4),
                    K.S_sb[:].rearrange("p (a b) -> p a b", a=4),
                    K.gcrow[:, grp * 4:(grp + 1) * 4]
                        .rearrange("p (a o) -> p a o", o=1).broadcast_to((128, 4, 128)),
                    op=ALU.mult)
                v.tensor_tensor(K.rets, K.rets, K.pB[:, 0:512], op=ALU.add)
                v.tensor_tensor(K.S_sb[:], K.rsq, K.pD[:, 0:512], op=ALU.add)
                v.tensor_reduce(
                    K.gnsum[:, ci * 8 + grp * 4:ci * 8 + grp * 4 + 4]
                        .rearrange("p (a o) -> p a o", o=1),
                    K.rets.rearrange("p (a b) -> p a b", a=4),
                    axis=AX.X, op=ALU.add)
                sV.inc(v.last)

            @blk.scalar
            def _(s, ci=ci, grp=grp):
                s.wait_ge(sV.h, sV.n)
                s.activation(K.rsq, K.rets, AF.Square)
                sS.inc(s.last)

            @blk.vector
            def _(v, ci=ci, grp=grp):
                v.wait_ge(sS.h, sS.n)
                if retgn_free[0] is not None:
                    v.wait_ge(sL.h, retgn_free[0])
                v.tensor_reduce(
                    K.gnsq[:, ci * 8 + grp * 4:ci * 8 + grp * 4 + 4]
                        .rearrange("p (a o) -> p a o", o=1),
                    K.rsq.rearrange("p (a b) -> p a b", a=4),
                    axis=AX.X, op=ALU.add)
                v.tensor_copy(K.retgn[:], K.rets)
                sV.inc(v.last)
            gn_n = sV.n

            @blk.sync
            def _(sy, ci=ci, grp=grp, gn_n=gn_n):
                sy.wait_ge(sV.h, gn_n)
                sL.dma(sy.dma_start(
                    out=K.rettm[ci * 128:(ci + 1) * 128, grp * 512:(grp + 1) * 512],
                    in_=K.retgn[:]))
            retgn_free[0] = sL.n

    @blk.vector
    def _(v):
        v.wait_ge(sL.h, sL.n)
        v.tensor_scalar_mul(K.gnmu[:], K.gnsum[:], 1.0 / DH)
        v.tensor_scalar(K.gnsq[:], K.gnsq[:], 1.0 / DH, EPS,
                        op0=ALU.mult, op1=ALU.add)
        v.tensor_tensor(K.gnsum[:], K.gnmu[:], K.gnmu[:], op=ALU.mult)
        vgap(v, K)
        v.tensor_tensor(K.gnsq[:], K.gnsq[:], K.gnsum[:], op=ALU.subtract)
        vgap(v, K)
        emit_rsqrt(v, K, K.gnrs[:], K.gnsq[:], 128, K.nw1[:])
        sV.inc(v.last)
    gnstats_done = sV.n

    lnt_free2 = [None, None]
    for i in range(NT):
        b = i % 2

        @blk.sync
        def _(sy, i=i, b=b):
            sy.wait_ge(sV.h, gnstats_done)
            if lnt_free2[b] is not None:
                sy.wait_ge(sL.h, lnt_free2[b])
            sL.dma(sy.dma_start(out=K.lnt[b][:], in_=K.rettm[i * 128:(i + 1) * 128, :]))
        ld_n = sL.n

        @blk.vector
        def _(v, i=i, b=b, ld_n=ld_n):
            v.wait_ge(sL.h, ld_n)
            v.tensor_tensor(
                K.lnt[b][:].rearrange("p (a b2) -> p a b2", a=8),
                K.lnt[b][:].rearrange("p (a b2) -> p a b2", a=8),
                K.gnmu[:, i * 8:(i + 1) * 8].rearrange("p (a o) -> p a o", o=1)
                    .broadcast_to((128, 8, 128)),
                op=ALU.subtract)
            vgap(v, K)
            v.tensor_tensor(
                K.lnt[b][:].rearrange("p (a b2) -> p a b2", a=8),
                K.lnt[b][:].rearrange("p (a b2) -> p a b2", a=8),
                K.gnrs[:, i * 8:(i + 1) * 8].rearrange("p (a o) -> p a o", o=1)
                    .broadcast_to((128, 8, 128)),
                op=ALU.mult)
            sV.inc(v.last)

        @blk.sync
        def _(sy, i=i, b=b):
            sy.wait_ge(sV.h, sV.n)
            sL.dma(sy.dma_start(out=K.rettm[i * 128:(i + 1) * 128, :], in_=K.lnt[b][:]))
        lnt_free2[b] = sL.n


def emit_finale(K):
    blk = K.block
    sL = K.sL

    @blk.sync
    def _(sy):
        sy.wait_ge(sL.h, sL.n)
        sy.wait_ge(K.sSc.h, K.sSc.n)


def kernel(L0_attn_x, L0_attn_Wq, L0_attn_Wk, L0_attn_Wv, L0_attn_W1, L0_attn_W2,
           L0_attn_gn_w, L0_attn_gn_b, L0_ffn_Wg, L0_ffn_bg, L0_ffn_W1, L0_ffn_b1,
           L0_ffn_W2, L0_ffn_b2):
    global _DBG_RES
    x = np.ascontiguousarray(np.asarray(L0_attn_x, dtype=np.float32))
    wq = np.ascontiguousarray(np.asarray(L0_attn_Wq, np.float32))
    wk = np.ascontiguousarray(np.asarray(L0_attn_Wk, np.float32))
    wv = np.ascontiguousarray(np.asarray(L0_attn_Wv, np.float32))
    w1a = np.ascontiguousarray(np.asarray(L0_attn_W1, np.float32))
    w2a = np.ascontiguousarray(np.asarray(L0_attn_W2, np.float32))
    gnw = np.asarray(L0_attn_gn_w, np.float32)
    gnb = np.asarray(L0_attn_gn_b, np.float32)
    wg = np.ascontiguousarray(np.asarray(L0_ffn_Wg, np.float32))
    bg = np.asarray(L0_ffn_bg, np.float32)
    w1 = np.ascontiguousarray(np.asarray(L0_ffn_W1, np.float32))
    b1 = np.asarray(L0_ffn_b1, np.float32)
    w2 = np.ascontiguousarray(np.asarray(L0_ffn_W2, np.float32))
    b2 = np.asarray(L0_ffn_b2, np.float32)

    nc = bass.Bass("TRN2", target_bir_lowering=False, detect_race_conditions=False)
    build(nc)

    gnw_t = np.ascontiguousarray(gnw.reshape(DC, 128).T)
    gnb_t = np.ascontiguousarray(gnb.reshape(DC, 128).T)
    bg_t = np.ascontiguousarray(bg.reshape(1, E))

    in_maps = []
    for c in range(NCORES):
        b1_t = np.ascontiguousarray(
            b1[2 * c:2 * c + 2].reshape(2, HC, 128).transpose(2, 0, 1).reshape(128, 2 * HC))
        b2_t = np.ascontiguousarray(b2[2 * c:2 * c + 2].reshape(1, 2 * D))
        tokp1 = np.ascontiguousarray(
            (np.arange(S).reshape(NT, 128).T + 2048.0 * c + 1.0).astype(np.float32))
        coresel = (np.arange(NCORES) < c).astype(np.float32).reshape(NCORES, 1)
        in_maps.append({
            "x_in": np.ascontiguousarray(x[c]),
            "wq_in": wq, "wk_in": wk, "wv_in": wv,
            "w1a_in": w1a, "w2a_in": w2a,
            "gnw_in": gnw_t, "gnb_in": gnb_t,
            "wg_in": wg, "bg_in": bg_t,
            "myw1_in": np.ascontiguousarray(w1[2 * c:2 * c + 2]),
            "myb1_in": b1_t,
            "myw2_in": np.ascontiguousarray(w2[2 * c:2 * c + 2]),
            "myb2_in": b2_t,
            "tokp1_in": tokp1,
            "coresel_in": coresel,
            "cid_in": np.array([[c]], dtype=np.uint32),
            "invsel_in": (16 * c + np.arange(16, dtype=np.uint32)).reshape(16, 1),
            "dmask_in": DMASK_HOST, "qdec_in": QDEC_HOST, "kdec_in": KDEC_HOST,
        })

    res = run_bass_kernel_spmd(nc, in_maps, core_ids=list(range(NCORES)))
    if _DBG:
        _DBG_RES = [{k: r.get("dbg_" + k) for k in _DBG} for r in res.results]
    out = np.stack([res.results[c]["out"] for c in range(NCORES)], axis=0)
    return out.astype(np.float32)
